# version 23
# baseline (speedup 1.0000x reference)
import numpy as np
from contextlib import ExitStack

import concourse.bass as bass
import concourse.mybir as mybir
from concourse.bass_utils import run_bass_kernel_spmd

F32 = mybir.dt.float32
BF16 = mybir.dt.bfloat16
AF = mybir.ActivationFunctionType
ALU = mybir.AluOpType
AX = mybir.AxisListType

P = 128
D = 1024
H = 16
DH = 64
M = 256
I3 = 3072
NCD = D // P
NCI = I3 // P
QMAX = 32767.0
MAGIC = float(1.5 * 2 ** 23)
WQMAX = 7.0
N_CORES = 8

ENGS = ("sync", "tensor", "vector", "scalar", "gpsimd")
NSLOT = 8
DMA_QUEUES = ("SD", "AD", "GD")
ALL_SEMS = tuple(f"{q}{i}" for q in DMA_QUEUES for i in range(NSLOT)) + (
    "T", "V", "A", "G", "CC", "SY")


class Buf:
    __slots__ = ("w", "r", "init", "name", "multi", "wl")

    def __init__(self, name="", init=(), multi=False):
        self.w = None
        self.r = []
        self.init = tuple(init)
        self.name = name
        self.multi = multi
        self.wl = []


class Em:
    def __init__(self):
        self.q = {e: [] for e in ENGS}
        self.cnt = {s: 0 for s in ALL_SEMS}
        self.mark = {}
        self.esem = {"sync": "SY", "tensor": "T", "vector": "V",
                     "scalar": "A", "gpsimd": "G"}
        self.dq = {"sync": "SD", "scalar": "AD", "gpsimd": "GD"}
        self.rot = {q: 0 for q in DMA_QUEUES}

    def fence(self):
        return [(s, self.cnt[s]) for s in ALL_SEMS if self.cnt[s] > 0]

    def _waits(self, eng, toks):
        best = {}
        for t in toks:
            if t is None:
                continue
            s, v = t
            if v > best.get(s, -1):
                best[s] = v
        out = []
        for s, v in best.items():
            k = (eng, s)
            if v > self.mark.get(k, 0):
                self.mark[k] = v
                out.append((s, v))
        return out

    def emit(self, eng, fns, rd=(), wr=(), sem=None, inc=None, extra=()):
        if sem is None:
            sem = self.esem[eng]
        if inc is None:
            inc = 1
        toks = []
        for b in rd:
            toks.append(b.w)
            toks.extend(b.wl)
            toks.extend(b.init)
        for b in wr:
            toks.append(b.w)
            toks.extend(b.r)
            toks.extend(b.init)
            if not b.multi:
                toks.extend(b.wl)
        toks.extend(extra)
        waits = self._waits(eng, toks)
        self.cnt[sem] += inc
        tok = (sem, self.cnt[sem])
        self.q[eng].append((waits, list(fns), sem, inc))
        for b in rd:
            b.r.append(tok)
        for b in wr:
            if b.multi:
                b.wl.append(tok)
            else:
                b.w = tok
                b.r = []
                b.init = ()
        return tok

    def op(self, eng, fn, rd=(), wr=(), sem=None, inc=None, extra=()):
        return self.emit(eng, [fn], rd=rd, wr=wr, sem=sem, inc=inc, extra=extra)

    def dma(self, eng, fn, rd=(), wr=(), extra=()):
        q = self.dq[eng]
        sem = f"{q}{self.rot[q]}"
        self.rot[q] = (self.rot[q] + 1) % NSLOT
        ex = list(extra)
        if self.cnt[sem] > 0:
            ex.append((sem, self.cnt[sem]))
        return self.emit(eng, [fn], rd=rd, wr=wr, sem=sem, inc=16, extra=ex)

    def run(self, block, sems):
        def body(e, items):
            for waits, fns, sem, inc in items:
                for s, v in waits:
                    e.wait_ge(sems[s], v)
                ins = None
                for fn in fns:
                    ins = fn(e)
                if ins is not None and inc:
                    ins.then_inc(sems[sem], inc)
        for eng in ENGS:
            getattr(block, eng)(lambda e, it=tuple(self.q[eng]): body(e, it))


def _host_consts():
    ident = np.eye(P, dtype=np.float32)
    nmask = np.zeros((P, NCD * H), dtype=np.float32)
    for c in range(NCD):
        nmask[0:64, c * H + 2 * c] = -0.5
        nmask[64:128, c * H + 2 * c + 1] = -0.5
    return ident, nmask


def build_nc(T):
    assert T % 512 == 0
    NT = T // 512
    TA = 256
    NTA = T // TA
    NTG = T // P
    DW = min(512, T)
    NDW = T // DW

    nc = bass.Bass("TRN2", target_bir_lowering=False)

    x_d = nc.dram_tensor("x", [T, D], F32, kind="ExternalInput").ap()
    Wqkv_d = nc.dram_tensor("Wqkv", [D, 3 * D], F32, kind="ExternalInput").ap()
    bqkv_d = nc.dram_tensor("bqkv", [3 * D], F32, kind="ExternalInput").ap()
    Wout_d = nc.dram_tensor("Wout", [D, D], F32, kind="ExternalInput").ap()
    bout_d = nc.dram_tensor("bout", [D], F32, kind="ExternalInput").ap()
    omega_d = nc.dram_tensor("omega", [DH, M], F32, kind="ExternalInput").ap()
    wgate_d = nc.dram_tensor("w_gate", [I3, D], F32, kind="ExternalInput").ap()
    wup_d = nc.dram_tensor("w_up", [I3, D], F32, kind="ExternalInput").ap()
    wdown_d = nc.dram_tensor("w_down", [D, I3], F32, kind="ExternalInput").ap()
    ident_d = nc.dram_tensor("ident", [P, P], F32, kind="ExternalInput").ap()
    nmask_d = nc.dram_tensor("nmask", [P, NCD * H], F32, kind="ExternalInput").ap()
    out_d = nc.dram_tensor("out", [T, D], F32, kind="ExternalOutput").ap()

    xnh_sp = nc.dram_tensor("xnh_sp", [NCD, P, T], BF16).ap()
    xnl_sp = nc.dram_tensor("xnl_sp", [NCD, P, T], BF16).ap()
    up_sp = nc.dram_tensor("up_sp", [NCI, P, T], F32).ap()
    gate_sp = nc.dram_tensor("gate_sp", [NCI, P, T], F32).ap()
    g2_sp = nc.dram_tensor("g2_sp", [NCI, P, T], F32).ap()
    o_sp = nc.dram_tensor("o_sp", [NCI, P, T], F32).ap()
    att_sp = nc.dram_tensor("att_sp", [T, D], F32).ap()
    wdTh_sp = nc.dram_tensor("wdTh_sp", [NCI, P, D], BF16).ap()
    wdTl_sp = nc.dram_tensor("wdTl_sp", [NCI, P, D], BF16).ap()
    wgTh_sp = nc.dram_tensor("wgTh_sp", [NCD, P, I3], BF16).ap()
    wgTl_sp = nc.dram_tensor("wgTl_sp", [NCD, P, I3], BF16).ap()
    cc_in = [nc.dram_tensor(f"cc_in{i}", [1, 2], F32) for i in range(4)]
    cc_out = [nc.dram_tensor(f"cc_out{i}", [1, 2], F32) for i in range(4)]
    xn_spB = Buf("xn_sp", multi=True)
    up_spB = Buf("up_sp", multi=True)
    gate_spB = Buf("gate_sp", multi=True)
    g2_spB = Buf("g2_sp", multi=True)
    o_spB = Buf("o_sp", multi=True)
    att_spB = Buf("att_sp", multi=True)
    wdT_spB = Buf("wdT_sp", multi=True)

    st = ExitStack()

    def sbt(name, shape, dtype=F32):
        return st.enter_context(nc.sbuf_tensor(name, shape, dtype))

    ident_sb = sbt("ident_sb", [P, P])
    nmask_sb = sbt("nmask_sb", [P, NCD * H])
    omg_sb = sbt("omg_sb", [P, M])
    ones_sb = sbt("ones_sb", [1, P])
    bk_sb = sbt("bk_sb", [P, NCD])
    bq_sb = sbt("bq_sb", [P, NCD])
    bv_sb = sbt("bv_sb", [1, D])
    bo_sb = sbt("bo_sb", [1, D])
    consts_sb = sbt("consts_sb", [P, 4])
    kvl_sb = sbt("kvl_sb", [P, H * 2 * 65])
    SC_N = 20
    sc_sb = sbt("sc_sb", [P, SC_N])
    pack_sb = sbt("pack_sb", [1, 8])

    def scal(j):
        return sc_sb[:, j:j + 1]

    (S_AM1, S_AM2, S_AM3, S_AM4, S_AM5,
     S_S1, S_IS1, S_S2, S_IS2, S_S3, S_S3H, S_IS3,
     S_S4, S_C4, S_S5, S_C5, S_TMP1) = range(17)
    scB = [Buf(f"sc{j}") for j in range(SC_N)]
    packB = Buf("pack")

    WPB_B = 98304
    WQB_B = 65536
    BIG_B = WPB_B + WQB_B
    big_sb = sbt("big_sb", [P, BIG_B], mybir.dt.uint8)
    big_base = nc.lookup_mloc(big_sb).addr
    AR_EL = 7000
    ar_sb = sbt("ar_sb", [P, AR_EL])

    _uid = [0]

    def place(shape, dtype, off):
        _uid[0] += 1
        h = nc.virtual_sbuf_tensor(f"v{_uid[0]}", list(shape), dtype)
        nc._tensor(h.name, list(h.shape), h.dtype, type="SB")
        ml = nc.lookup_mloc(h)
        ml.allocated = True
        ml.addr = big_base + off
        return h.ap()

    def mkplace(*ranges):
        state = [[r[0], r[1]] for r in ranges]

        def alloc(shape, dtype):
            nb = int(np.prod(shape[1:])) * mybir.dt.size(dtype)
            nb = (nb + 31) // 32 * 32
            for s in state:
                if s[1] >= nb:
                    off = s[0]
                    s[0] += nb
                    s[1] -= nb
                    return place(shape, dtype, off)
            raise AssertionError(f"place overflow: need {nb}")
        return alloc

    def mkar():
        stt = [0]

        def alloc(nel):
            o = stt[0]
            stt[0] += nel
            assert stt[0] <= AR_EL, f"ar overflow {stt[0]}"
            return ar_sb[:, o:o + nel]
        return alloc

    wPh = place([P, NCI * D], BF16, 0)
    wPl = place([P, NCI * D], BF16, WPB_B // 2)
    wQh = place([P, NCD * 2 * D], BF16, WPB_B)
    wQl = place([P, NCD * 2 * D], BF16, WPB_B + WQB_B // 2)
    wPB = Buf("wP")
    wQB = Buf("wQ")

    def refence(bufs, f):
        for b in bufs:
            b.init = tuple(f)

    pbank = [st.enter_context(nc.psum_tensor(f"pb{i}", [P, 512], F32))
             for i in range(8)]
    pbB = [Buf(f"pb{i}") for i in range(8)]

    E = Em()
    IDB, NMB, OMB, ONB = Buf("id"), Buf("nm"), Buf("om"), Buf("on")
    BKB, BQB, BVB, BOB = Buf("bk"), Buf("bq"), Buf("bv"), Buf("bo")
    CSTB = Buf("cst")
    KVLB = Buf("kvl")

    E.dma("sync", lambda e: e.dma_start(out=ident_sb[:], in_=ident_d), wr=[IDB])
    E.dma("sync", lambda e: e.dma_start(out=nmask_sb[:], in_=nmask_d), wr=[NMB])
    E.dma("sync", lambda e: e.dma_start(out=omg_sb[0:64, :], in_=omega_d), wr=[OMB])
    E.dma("sync", lambda e: e.dma_start(out=omg_sb[64:128, :], in_=omega_d), wr=[OMB])
    bq3 = bqkv_d.rearrange("(s c p) -> s p c", s=3, c=NCD)
    E.dma("sync", lambda e: e.dma_start(out=bq_sb[:], in_=bq3[0],
                                        allow_slow_non_contiguous=True), wr=[BQB])
    E.dma("sync", lambda e: e.dma_start(out=bk_sb[:], in_=bq3[1],
                                        allow_slow_non_contiguous=True), wr=[BKB])
    E.dma("sync", lambda e: e.dma_start(out=bv_sb[0:1, :],
                                        in_=bqkv_d[2 * D:3 * D][None, :]), wr=[BVB])
    E.dma("sync", lambda e: e.dma_start(out=bo_sb[0:1, :], in_=bout_d[None, :]),
          wr=[BOB])
    E.op("vector", lambda e: e.memset(ones_sb[:], 1.0), wr=[ONB])
    E.op("vector", lambda e: e.memset(consts_sb[:, 0:1], 1e-5), wr=[CSTB])
    E.op("vector", lambda e: e.memset(consts_sb[:, 1:2], float(np.log(1e-6))),
         wr=[CSTB])
    E.op("vector", lambda e: e.memset(consts_sb[:, 2:3], MAGIC), wr=[CSTB])
    E.op("vector", lambda e: e.memset(consts_sb[:, 3:4], -MAGIC), wr=[CSTB])
    E.op("vector", lambda e: e.memset(pack_sb[:], 0.0), wr=[packB])
    eps_ap = lambda: consts_sb[:, 0:1]
    ln6_ap = lambda n=P: consts_sb[0:n, 1:2]
    mgp_ap = lambda: consts_sb[:, 2:3]
    mgn_ap = lambda: consts_sb[:, 3:4]

    def split_pair(src_ap, hi_ap, lo_ap, rd, hiB, loB, eng_hi="scalar"):
        if eng_hi == "scalar":
            E.op("scalar", lambda e: e.activation(hi_ap, src_ap, AF.Identity,
                                                  bias=0.0, scale=1.0),
                 rd=list(rd), wr=[hiB])
        else:
            E.op("vector", lambda e: e.tensor_copy(hi_ap, src_ap),
                 rd=list(rd), wr=[hiB])
        E.op("vector", lambda e: e.tensor_tensor(lo_ap, src_ap, hi_ap,
                                                 ALU.subtract),
             rd=list(rd) + [hiB], wr=[loB])

    def mm3(bank, lT_h, lT_l, r_h, r_l, first, last, n0=0, n1=512):
        return [
            lambda e: e.matmul(pbank[bank][:, n0:n1], lT_h, r_h,
                               start=first, stop=False),
            lambda e: e.matmul(pbank[bank][:, n0:n1], lT_l, r_h,
                               start=False, stop=False),
            lambda e: e.matmul(pbank[bank][:, n0:n1], lT_h, r_l,
                               start=False, stop=last),
        ]

    def sc_ts(dstj, srcj, s1, op0, s2=None, op1=None):
        E.op("vector", lambda e: e.tensor_scalar(
            scal(dstj), scal(srcj), s1, s2, op0,
            op1 if op1 is not None else ALU.bypass),
            rd=[scB[srcj]], wr=[scB[dstj]])

    def sc_recip(dstj, srcj):
        E.op("vector", lambda e: e.reciprocal(scal(dstj), scal(srcj)),
             rd=[scB[srcj]], wr=[scB[dstj]])

    def sc_mul(dstj, aj, bj):
        E.op("vector", lambda e: e.tensor_tensor(
            scal(dstj), scal(aj), scal(bj), ALU.mult),
            rd=[scB[aj], scB[bj]], wr=[scB[dstj]])

    def global_absmax(idx, slots, adv=None):
        if adv is not None:
            adv(6)
        for j, sj in enumerate(slots):
            E.op("tensor", lambda e, sj=sj: e.transpose(
                pbank[0][0:1, 0:P], scal(sj), ident_sb[:]),
                rd=[scB[sj], IDB], wr=[pbB[0]])
            E.op("vector", lambda e, j=j: e.tensor_reduce(
                pack_sb[0:1, j:j + 1], pbank[0][0:1, 0:P], AX.X, ALU.max,
                apply_absolute_value=True), rd=[pbB[0]], wr=[packB])
        gd_tok = E.dma("gpsimd", lambda e, idx=idx: e.dma_start(
            out=cc_in[idx].ap()[0:1, 0:2], in_=pack_sb[0:1, 0:2]), rd=[packB])
        E.op("gpsimd", lambda e, idx=idx: e.collective_compute(
            "AllReduce", ALU.max, replica_groups=[list(range(N_CORES))],
            ins=[cc_in[idx].ap().opt()], outs=[cc_out[idx].ap().opt()]),
            sem="CC", inc=1, extra=[gd_tok])
        cc_tok = ("CC", E.cnt["CC"])
        E.dma("gpsimd", lambda e, idx=idx: e.dma_start(
            out=pack_sb[0:1, 0:2], in_=cc_out[idx].ap()[0:1, 0:2]),
            wr=[packB], extra=[cc_tok])
        if adv is not None:
            adv(8)
        for j, sj in enumerate(slots):
            E.op("tensor", lambda e, j=j: e.matmul(
                pbank[0][:, 0:1], ones_sb[0:1, 0:P], pack_sb[0:1, j:j + 1],
                start=True, stop=True), rd=[packB, ONB], wr=[pbB[0]])
            E.op("vector", lambda e, sj=sj: e.tensor_copy(
                scal(sj), pbank[0][:, 0:1]), rd=[pbB[0]], wr=[scB[sj]])

    def rne2(eng, dst, src, sc1, rd, dstB):
        E.op(eng, lambda e: e.tensor_scalar(dst, src, sc1, MAGIC, ALU.mult,
                                            ALU.add), rd=rd, wr=[dstB])
        E.op(eng, lambda e: e.tensor_scalar(dst, dst, MAGIC, None,
                                            ALU.subtract), rd=[], wr=[dstB])

    def rne2_act(dst, src_, sc1_ap, rd, dstB):
        E.op("scalar", lambda e: e.activation(
            dst, src_, AF.Identity, bias=mgp_ap(), scale=sc1_ap),
            rd=list(rd) + [CSTB], wr=[dstB])
        E.op("scalar", lambda e: e.activation(
            dst, dst, AF.Identity, bias=mgn_ap(), scale=1.0),
            rd=[CSTB], wr=[dstB])

    def rne2_g(dst, src_, sc1_ap, rd, dstB):
        E.op("gpsimd", lambda e: e.tensor_scalar(dst, src_, sc1_ap, MAGIC,
                                                 ALU.mult, ALU.add),
             rd=list(rd), wr=[dstB])
        E.op("gpsimd", lambda e: e.tensor_scalar(dst, dst, 1.0, -MAGIC,
                                                 ALU.mult, ALU.add),
             rd=[], wr=[dstB])

    def clip_g(dst, lim_hi, lim_lo, dstB):
        E.op("gpsimd", lambda e: e.tensor_scalar(dst, dst, lim_hi, lim_lo,
                                                 ALU.min, ALU.max),
             rd=[], wr=[dstB])

    al = mkplace((0, WPB_B))
    arA = mkar()
    kvacc = al([P, H * M], F32)
    KVAB = Buf("kva")
    a_x = [al([P, D], F32) for _ in range(4)]
    a_xB = [Buf(f"a_x{i}") for i in range(4)]
    a_st = arA(16)
    a_stB = Buf("a_st")
    a_xn = al([P, NCD * 512], F32)
    a_xnB = [Buf(f"a_xn_{c}") for c in range(NCD)]
    a_xnh = al([P, NCD * 512], BF16)
    a_xnl = al([P, NCD * 512], BF16)
    a_xnpB = [Buf(f"a_xnp_{c}") for c in range(NCD)]
    a_k = al([P, NCD * 512], F32)
    a_kB = [Buf(f"a_k{c}") for c in range(NCD)]
    a_ksq = arA(NCD * 512)
    a_ksqB = [Buf(f"a_ksq{c}") for c in range(NCD)]
    a_nsq = arA(64)
    a_nsqB = [Buf(f"a_nsq{s}") for s in range(4)]
    a_kp = [arA(M) for _ in range(2)]
    a_kpB = [Buf("a_kp0"), Buf("a_kp1")]
    a_va = [al([P, H * 65], F32) for _ in range(3)] + [arA(H * 65)]
    a_vaB = [Buf(f"a_va{s}") for s in range(4)]
    a_am = arA(2)
    a_amB = Buf("a_am")
    a_amtB = Buf("a_amt")
    a_wst = None
    a_wstB = Buf("a_wst")

    a_wst = a_ksq[:, 0:2048]
    for c in range(NCD):
        E.dma("sync", lambda e, c=c: e.dma_start(
            out=a_wst[:, :], in_=Wqkv_d[c * P:(c + 1) * P, D:3 * D]),
            wr=[a_wstB])
        split_pair(a_wst[:, :], wQh[:, c * 2 * D:(c + 1) * 2 * D],
                   wQl[:, c * 2 * D:(c + 1) * 2 * D],
                   [a_wstB], wQB, wQB)
    for c in range(NCD):
        a_ksqB[c].init = (("V", E.cnt["V"]), ("A", E.cnt["A"]))

    def wQs(t, kc, which, j0, j1):
        base = kc * 2 * D + which * D
        return t[:, base + j0: base + j1]

    E.op("vector", lambda e: e.memset(kvacc[:, :], 0.0), wr=[KVAB])
    E.op("vector", lambda e: e.memset(a_am[:, 0:1], 0.0), wr=[a_amB])
    for s in range(4):
        E.op("vector", lambda e, s=s: e.memset(a_va[s][:, :], 1.0),
             wr=[a_vaB[s]])

    def a_ln(it):
        t0 = it * 512
        for s in range(4):
            tt = t0 + s * 128
            xb = s
            E.dma("sync", lambda e, tt=tt, xb=xb: e.dma_start(
                out=a_x[xb][:, :], in_=x_d[tt:tt + P, :]), wr=[a_xB[xb]])
            E.op("vector", lambda e, xb=xb: e.bn_stats(
                a_st[:, 0:6], a_x[xb][:, 0:512]),
                rd=[a_xB[xb]], wr=[a_stB])
            E.op("vector", lambda e, xb=xb: e.bn_stats(
                a_st[:, 6:12], a_x[xb][:, 512:1024]),
                rd=[a_xB[xb]], wr=[a_stB])
            E.op("vector", lambda e: e.bn_aggr(a_st[:, 12:14], a_st[:, 0:12]),
                 rd=[a_stB], wr=[a_stB])
            E.op("scalar", lambda e: e.activation(
                a_st[:, 14:15], a_st[:, 13:14], AF.Ln, bias=eps_ap(),
                scale=1.0), rd=[a_stB, CSTB], wr=[a_stB])
            E.op("scalar", lambda e: e.activation(
                a_st[:, 15:16], a_st[:, 14:15], AF.Exp, bias=0.0, scale=-0.5),
                rd=[a_stB], wr=[a_stB])
            E.op("vector", lambda e, xb=xb: e.tensor_scalar(
                a_x[xb][:, :], a_x[xb][:, :], a_st[:, 12:13],
                a_st[:, 15:16], ALU.subtract, ALU.mult),
                rd=[a_stB], wr=[a_xB[xb]])
            E.op("vector", lambda e, xb=xb: e.tensor_reduce(
                a_am[:, 1:2], a_x[xb][:, :], AX.X, ALU.max,
                apply_absolute_value=True), rd=[a_xB[xb]], wr=[a_amtB])
            E.op("vector", lambda e: e.tensor_tensor(
                a_am[:, 0:1], a_am[:, 0:1], a_am[:, 1:2], ALU.max),
                rd=[a_amtB], wr=[a_amB])

    def a_tr(it):
        t0 = it * 512
        for s in range(4):
            xb = s
            for c in range(NCD):
                bk = (s * NCD + c) % 2
                E.op("tensor", lambda e, xb=xb, c=c, bk=bk: e.transpose(
                    pbank[bk][:, 0:P], a_x[xb][:, c * P:(c + 1) * P],
                    ident_sb[:]), rd=[a_xB[xb], IDB], wr=[pbB[bk]])
                E.op("scalar", lambda e, c=c, s=s, bk=bk: e.activation(
                    a_xn[:, c * 512 + s * 128:c * 512 + (s + 1) * 128],
                    pbank[bk][:, 0:P], AF.Identity, bias=0.0, scale=1.0),
                    rd=[pbB[bk]], wr=[a_xnB[c]])
        for c in range(NCD):
            split_pair(a_xn[:, c * 512:(c + 1) * 512],
                       a_xnh[:, c * 512:(c + 1) * 512],
                       a_xnl[:, c * 512:(c + 1) * 512],
                       [a_xnB[c]], a_xnpB[c], a_xnpB[c])
        E.dma("scalar", lambda e, t0=t0: e.dma_start(
            out=xnh_sp[:, :, t0:t0 + 512].rearrange("c p t -> p c t"),
            in_=a_xnh[:, :].rearrange("p (c t) -> p c t", c=NCD)),
            rd=a_xnpB, wr=[xn_spB])
        E.dma("scalar", lambda e, t0=t0: e.dma_start(
            out=xnl_sp[:, :, t0:t0 + 512].rearrange("c p t -> p c t"),
            in_=a_xnl[:, :].rearrange("p (c t) -> p c t", c=NCD)),
            rd=a_xnpB, wr=[xn_spB])

    a_ln(0)
    a_tr(0)
    for it in range(NT):
        t0 = it * 512
        if it + 1 < NT:
            a_ln(it + 1)
        for c in range(NCD):
            bk = 2 + c % 2
            fns = []
            for kc in range(NCD):
                fns += mm3(bk, wQs(wQh, kc, 0, c * P, (c + 1) * P),
                           wQs(wQl, kc, 0, c * P, (c + 1) * P),
                           a_xnh[:, kc * 512:(kc + 1) * 512],
                           a_xnl[:, kc * 512:(kc + 1) * 512],
                           first=(kc == 0), last=(kc == NCD - 1))
            E.emit("tensor", fns, rd=[wQB] + a_xnpB, wr=[pbB[bk]])
            E.op("scalar", lambda e, c=c, bk=bk: e.activation(
                a_k[:, c * 512:(c + 1) * 512], pbank[bk][:, :],
                AF.Identity, bias=bk_sb[:, c:c + 1], scale=1.0),
                rd=[pbB[bk], BKB], wr=[a_kB[c]])
            E.op("vector", lambda e, c=c: e.tensor_tensor(
                a_ksq[:, c * 512:(c + 1) * 512],
                a_k[:, c * 512:(c + 1) * 512],
                a_k[:, c * 512:(c + 1) * 512], ALU.mult),
                rd=[a_kB[c]], wr=[a_ksqB[c]])
        for s in range(4):
            bk = 2 + s % 2
            fns = [lambda e, c=c, s=s, bk=bk: e.matmul(
                pbank[bk][:, 0:16],
                a_ksq[:, c * 512 + s * 128:c * 512 + (s + 1) * 128],
                nmask_sb[:, c * 16:(c + 1) * 16],
                start=(c == 0), stop=(c == NCD - 1)) for c in range(NCD)]
            E.emit("tensor", fns, rd=a_ksqB + [NMB], wr=[pbB[bk]])
            E.op("scalar", lambda e, s=s, bk=bk: e.activation(
                a_nsq[:, s * 16:(s + 1) * 16], pbank[bk][:, 0:16], AF.Identity,
                bias=0.0, scale=1.0), rd=[pbB[bk]], wr=[a_nsqB[s]])
        for s in range(4):
            for half in range(2):
                bk = 4 + (s * 2 + half) % 2
                fns = []
                for kc in range(NCD):
                    fns += mm3(
                        bk,
                        a_xnh[:, kc * 512 + s * 128:kc * 512 + (s + 1) * 128],
                        a_xnl[:, kc * 512 + s * 128:kc * 512 + (s + 1) * 128],
                        wQs(wQh, kc, 1, half * 512, (half + 1) * 512),
                        wQs(wQl, kc, 1, half * 512, (half + 1) * 512),
                        first=(kc == 0), last=False)
                fns.append(lambda e, half=half, bk=bk: e.matmul(
                    pbank[bk][:, :], ones_sb[0:1, 0:P],
                    bv_sb[0:1, half * 512:(half + 1) * 512],
                    start=False, stop=True))
                E.emit("tensor", fns, rd=[wQB, ONB, BVB] + a_xnpB,
                       wr=[pbB[bk]])
                E.op("scalar", lambda e, s=s, half=half, bk=bk: e.activation(
                    a_va[s][:, :].rearrange("p (h g) -> p h g", g=65)[
                        :, half * 8:(half + 1) * 8, 0:64],
                    pbank[bk][:, :].rearrange("p (h g) -> p h g", g=64),
                    AF.Identity, bias=0.0, scale=1.0),
                    rd=[pbB[bk]], wr=[a_vaB[s]])
        if it + 1 < NT:
            a_tr(it + 1)
        def a_km(h, s):
            c = h // 2
            rlo = (h % 2) * 64
            kb = s % 2
            E.op("tensor", lambda e, c=c, rlo=rlo, s=s, kb=kb: e.matmul(
                pbank[kb][:, 0:M],
                a_k[rlo:rlo + 64, c * 512 + s * 128:c * 512 + (s + 1) * 128],
                omg_sb[rlo:rlo + 64, :], start=True, stop=True),
                rd=[a_kB[c], OMB], wr=[pbB[kb]])
            E.op("scalar", lambda e, h=h, s=s, kb=kb: e.activation(
                a_kp[kb][:, :], pbank[kb][:, 0:M], AF.Exp,
                bias=a_nsq[:, s * 16 + h:s * 16 + h + 1], scale=1.0),
                rd=[pbB[kb], a_nsqB[s]], wr=[a_kpB[kb]])

        def a_kv(h, s):
            kb = s % 2
            kvbk = 6 + h % 2
            E.op("tensor", lambda e, h=h, s=s, kb=kb, kvbk=kvbk: e.matmul(
                pbank[kvbk][0:65, 0:M],
                a_va[s][:, h * 65:(h + 1) * 65],
                a_kp[kb][:, :], start=(s == 0), stop=(s == 3)),
                rd=[a_vaB[s], a_kpB[kb]], wr=[pbB[kvbk]])
            if s == 3:
                E.op("vector", lambda e, h=h, kvbk=kvbk: e.tensor_tensor(
                    kvacc[0:65, h * M:(h + 1) * M],
                    kvacc[0:65, h * M:(h + 1) * M],
                    pbank[kvbk][0:65, 0:M], ALU.add),
                    rd=[pbB[kvbk]], wr=[KVAB])

        prev = None
        for h in range(H):
            for s in range(4):
                a_km(h, s)
                if prev is not None:
                    a_kv(*prev)
                prev = (h, s)
        a_kv(*prev)

    for h in range(H):
        for mc in range(2):
            bk = (h * 2 + mc) % 2
            E.op("tensor", lambda e, h=h, mc=mc, bk=bk: e.transpose(
                pbank[bk][:, 0:65],
                kvacc[0:65, h * M + mc * P:h * M + mc * P + P],
                ident_sb[0:65, 0:65]), rd=[KVAB, IDB], wr=[pbB[bk]])
            E.op("scalar", lambda e, h=h, mc=mc, bk=bk: e.activation(
                kvl_sb[:, (h * 2 + mc) * 65:(h * 2 + mc + 1) * 65],
                pbank[bk][:, 0:65], AF.Identity, bias=0.0, scale=1.0),
                rd=[pbB[bk]], wr=[KVLB])

    E.op("vector", lambda e: e.tensor_copy(scal(S_AM1), a_am[:, 0:1]),
         rd=[a_amB], wr=[scB[S_AM1]])
    global_absmax(0, [S_AM1])
    sc_ts(S_S1, S_AM1, 1.0 / QMAX, ALU.mult, 1e-12, ALU.max)
    sc_recip(S_IS1, S_S1)

    def dequant_transpose(w_ap, n_oc, n_k, dsth, dstl, stages, hbm=None,
                          ev_st=None):
        NBK = n_k // 32

        def chain(oc):
            stg, stgB = stages[oc % len(stages)]
            E.dma("sync", lambda e, oc=oc, stg=stg: e.dma_start(
                out=stg[:, 0:n_k], in_=w_ap[oc * P:(oc + 1) * P, :]), wr=[stgB])
            E.op("vector", lambda e, stg=stg: e.tensor_reduce(
                stg[:, n_k:n_k + NBK],
                stg[:, 0:n_k].rearrange("p (nb b) -> p nb b", b=32),
                AX.X, ALU.max, apply_absolute_value=True), rd=[], wr=[stgB])
            sA = stg[:, n_k:n_k + NBK]
            sR = stg[:, n_k + NBK:n_k + 2 * NBK]
            sS = stg[:, n_k + 2 * NBK:n_k + 3 * NBK]
            E.op("vector", lambda e, sA=sA, sS=sS: e.tensor_scalar(
                sS, sA, 1.0 / 7.0, None, ALU.mult), rd=[], wr=[stgB])
            E.op("vector", lambda e, sR=sR, sS=sS: e.tensor_scalar(
                sR, sS, -7.0, None, ALU.mult), rd=[], wr=[stgB])
            E.op("vector", lambda e, sA=sA, sR=sR: e.tensor_tensor(
                sR, sA, sR, ALU.add), rd=[], wr=[stgB])
            E.op("vector", lambda e, sR=sR: e.tensor_scalar(
                sR, sR, 1.0 / 7.0, None, ALU.mult), rd=[], wr=[stgB])
            E.op("vector", lambda e, sA=sA, sR=sR, sS=sS: e.tensor_tensor(
                sA, sS, sR, ALU.add), rd=[], wr=[stgB])
            E.op("vector", lambda e, sA=sA: e.tensor_scalar(
                sA, sA, 1e-12, None, ALU.max), rd=[], wr=[stgB])
            E.op("vector", lambda e, sA=sA, sR=sR: e.reciprocal(sR, sA),
                 rd=[], wr=[stgB])
            E.op("gpsimd", lambda e, stg=stg, sR=sR: e.tensor_tensor(
                stg[:, 0:n_k].rearrange("p (nb b) -> p nb b", b=32),
                stg[:, 0:n_k].rearrange("p (nb b) -> p nb b", b=32),
                sR[:, :, None].broadcast_to((P, NBK, 32)), ALU.mult),
                rd=[], wr=[stgB])
            E.op("gpsimd", lambda e, stg=stg: e.tensor_scalar(
                stg[:, 0:n_k], stg[:, 0:n_k], 1.0, MAGIC,
                ALU.mult, ALU.add), rd=[], wr=[stgB])
            E.op("gpsimd", lambda e, stg=stg: e.tensor_scalar(
                stg[:, 0:n_k], stg[:, 0:n_k], 1.0, -MAGIC,
                ALU.mult, ALU.add), rd=[], wr=[stgB])
            E.op("gpsimd", lambda e, stg=stg: e.tensor_scalar(
                stg[:, 0:n_k], stg[:, 0:n_k], WQMAX, -WQMAX - 1.0,
                ALU.min, ALU.max), rd=[], wr=[stgB])
            E.op("gpsimd", lambda e, stg=stg, sA=sA: e.tensor_tensor(
                stg[:, 0:n_k].rearrange("p (nb b) -> p nb b", b=32),
                stg[:, 0:n_k].rearrange("p (nb b) -> p nb b", b=32),
                sA[:, :, None].broadcast_to((P, NBK, 32)), ALU.mult),
                rd=[], wr=[stgB])

        def xpose(oc):
            stg, stgB = stages[oc % len(stages)]
            for kc in range(n_k // P):
                bk = 4 + kc % 4
                E.op("tensor", lambda e, stg=stg, kc=kc, bk=bk: e.transpose(
                    pbank[bk][:, 0:P], stg[:, kc * P:(kc + 1) * P],
                    ident_sb[:]), rd=[stgB, IDB], wr=[pbB[bk]])
                if hbm is None:
                    E.op("scalar", lambda e, kc=kc, oc=oc, bk=bk:
                         e.activation(dsth(kc, oc), pbank[bk][:, 0:P],
                                      AF.Identity, bias=0.0, scale=1.0),
                         rd=[pbB[bk]], wr=[wPB])
                    E.op("vector", lambda e, kc=kc, oc=oc, bk=bk:
                         e.tensor_tensor(dstl(kc, oc), pbank[bk][:, 0:P],
                                         dsth(kc, oc), ALU.subtract),
                         rd=[pbB[bk]], wr=[wPB])
                else:
                    evh, evl, evB = ev_st[kc % 2]
                    E.op("scalar", lambda e, evh=evh, bk=bk: e.activation(
                        evh[:, 0:P], pbank[bk][:, 0:P], AF.Identity,
                        bias=0.0, scale=1.0), rd=[pbB[bk]], wr=[evB])
                    E.op("vector", lambda e, evh=evh, evl=evl, bk=bk:
                         e.tensor_tensor(evl[:, 0:P], pbank[bk][:, 0:P],
                                         evh[:, 0:P], ALU.subtract),
                         rd=[pbB[bk]], wr=[evB])
                    hh, hl = hbm
                    E.dma("gpsimd", lambda e, evh=evh, kc=kc, oc=oc, hh=hh:
                          e.dma_start(out=hh[kc, :, oc * P:(oc + 1) * P],
                                      in_=evh[:, 0:P]),
                          rd=[evB], wr=[wdT_spB])
                    E.dma("gpsimd", lambda e, evl=evl, kc=kc, oc=oc, hl=hl:
                          e.dma_start(out=hl[kc, :, oc * P:(oc + 1) * P],
                                      in_=evl[:, 0:P]),
                          rd=[evB], wr=[wdT_spB])

        chain(0)
        for oc in range(1, n_oc + 1):
            if oc < n_oc:
                chain(oc)
            xpose(oc - 1)
            yield

    def wPu_h(kc, oc):
        return wPh[:, kc * I3 + oc * P:kc * I3 + (oc + 1) * P]

    def wPu_l(kc, oc):
        return wPl[:, kc * I3 + oc * P:kc * I3 + (oc + 1) * P]

    def phaseB(w_sp_dst, w_sp_dstB, amj, fence, alq, arB_):
        b_xnh = [alq([P, NCD * 512], BF16)]
        b_xnl = [alq([P, NCD * 512], BF16)]
        b_xnB = [Buf("b_xn0", fence)]
        b_q1h = [alq([P, NCD * 512], BF16) for _ in range(2)]
        b_q1l = [alq([P, NCD * 512], BF16) for _ in range(2)]
        b_q1B = [[Buf(f"b_q1{q}_{c}", fence) for c in range(NCD)]
                 for q in range(2)]
        b_f = alq([P, 2 * 512], F32)
        b_fB = [Buf("b_f0", fence), Buf("b_f1", fence)]
        b_ev = [alq([P, 2 * 512], F32) for _ in range(2)]
        b_evB = [[Buf(f"b_ev{g}_{j}", fence) for j in range(2)]
                 for g in range(2)]
        b_am = arB_(2)
        b_amB, b_amtB = Buf("b_am", fence), Buf("b_amt", fence)
        E.op("vector", lambda e: e.memset(b_am[:, 0:1], 0.0), wr=[b_amB])

        def b_chain(it):
            t0 = it * 512
            qb = it % 2
            E.dma("sync", lambda e, t0=t0: e.dma_start(
                out=b_xnh[0][:, :].rearrange("p (c t) -> p c t", c=NCD),
                in_=xnh_sp[:, :, t0:t0 + 512].rearrange("c p t -> p c t")),
                rd=[xn_spB], wr=[b_xnB[0]])
            E.dma("sync", lambda e, t0=t0: e.dma_start(
                out=b_xnl[0][:, :].rearrange("p (c t) -> p c t", c=NCD),
                in_=xnl_sp[:, :, t0:t0 + 512].rearrange("c p t -> p c t")),
                rd=[xn_spB], wr=[b_xnB[0]])
            for c in range(NCD):
                fs = c % 2
                E.op("vector", lambda e, c=c, fs=fs: e.tensor_tensor(
                    b_f[:, fs * 512:(fs + 1) * 512],
                    b_xnh[0][:, c * 512:(c + 1) * 512],
                    b_xnl[0][:, c * 512:(c + 1) * 512], ALU.add),
                    rd=[b_xnB[0]], wr=[b_fB[fs]])
                rne2("vector", b_f[:, fs * 512:(fs + 1) * 512],
                     b_f[:, fs * 512:(fs + 1) * 512], scal(S_IS1),
                     [scB[S_IS1]], b_fB[fs])
                clip_g(b_f[:, fs * 512:(fs + 1) * 512], QMAX, -QMAX - 1.0,
                       b_fB[fs])
                split_pair(b_f[:, fs * 512:(fs + 1) * 512],
                           b_q1h[qb][:, c * 512:(c + 1) * 512],
                           b_q1l[qb][:, c * 512:(c + 1) * 512],
                           [b_fB[fs]], b_q1B[qb][c], b_q1B[qb][c])

        b_chain(0)
        for it in range(NT):
            t0 = it * 512
            qb = it % 2
            if it + 1 < NT:
                b_chain(it + 1)
            for ic in range(NCI):
                bk = ic % 4
                g = (ic // 2) % 2
                j = ic % 2
                fns = []
                for kc in range(NCD):
                    fns += mm3(bk, wPu_h(kc, ic), wPu_l(kc, ic),
                               b_q1h[qb][:, kc * 512:(kc + 1) * 512],
                               b_q1l[qb][:, kc * 512:(kc + 1) * 512],
                               first=(kc == 0), last=(kc == NCD - 1))
                E.emit("tensor", fns, rd=[wPB] + b_q1B[qb], wr=[pbB[bk]])
                E.op("scalar", lambda e, bk=bk, g=g, j=j: e.activation(
                    b_ev[g][:, j * 512:(j + 1) * 512], pbank[bk][:, :],
                    AF.Identity, bias=0.0, scale=scal(S_S1)),
                    rd=[pbB[bk], scB[S_S1]], wr=[b_evB[g][j]])
                E.op("vector", lambda e, g=g, j=j: e.tensor_reduce(
                    b_am[:, 1:2], b_ev[g][:, j * 512:(j + 1) * 512], AX.X,
                    ALU.max, apply_absolute_value=True),
                    rd=[b_evB[g][j]], wr=[b_amtB])
                E.op("vector", lambda e: e.tensor_tensor(
                    b_am[:, 0:1], b_am[:, 0:1], b_am[:, 1:2], ALU.max),
                    rd=[b_amtB], wr=[b_amB])
                if j == 1:
                    c0 = ic - 1
                    E.dma("scalar", lambda e, c0=c0, t0=t0, g=g: e.dma_start(
                        out=w_sp_dst[c0:c0 + 2, :, t0:t0 + 512].rearrange(
                            "c p t -> p c t"),
                        in_=b_ev[g][:, :].rearrange("p (c t) -> p c t", c=2)),
                        rd=b_evB[g], wr=[w_sp_dstB])
                if ic == 11:
                    yield
            yield
        E.op("vector", lambda e: e.tensor_copy(scal(amj), b_am[:, 0:1]),
             rd=[b_amB], wr=[scB[amj]])

    fence = E.fence()
    refence([wPB], fence)
    arB = mkar()
    dq_st = [(arB(1024 + 96), Buf("dqA", fence)),
             (arB(1024 + 96), Buf("dqB", fence))]
    for _ in dequant_transpose(wup_d, NCI, D, wPu_h, wPu_l, dq_st):
        pass
    dq_std = [(arB(1536 + 144), Buf("dqD0", fence)),
              (arB(1536 + 144), Buf("dqD1", fence))]
    dq_ev = [(place([P, P], BF16, WPB_B + WQB_B - 1024),
              place([P, P], BF16, WPB_B + WQB_B - 768), Buf("dqE0", fence)),
             (place([P, P], BF16, WPB_B + WQB_B - 512),
              place([P, P], BF16, WPB_B + WQB_B - 256), Buf("dqE1", fence))]

    def genWG():
        yield from dequant_transpose(
            wgate_d, NCI, D, None, None, dq_st,
            hbm=(wgTh_sp, wgTl_sp), ev_st=dq_ev)

    def genWD():
        for half in range(2):
            yield from dequant_transpose(
                wdown_d[:, half * 1536:(half + 1) * 1536], NCD, 1536,
                None, None, dq_std,
                hbm=(wdTh_sp[half * 12:(half + 1) * 12],
                     wdTl_sp[half * 12:(half + 1) * 12]), ev_st=dq_ev)

    def drainB(g1, g2, r1, r2):
        done2 = False
        while True:
            try:
                for _ in range(r1):
                    next(g1)
            except StopIteration:
                break
            if not done2:
                try:
                    for _ in range(r2):
                        next(g2)
                except StopIteration:
                    done2 = True
        if not done2:
            for _ in g2:
                pass

    alq2 = mkplace((WPB_B, WQB_B - 1024))
    drainB(phaseB(up_sp, up_spB, S_AM2, fence, alq2, arB), genWG(), 1, 1)

    fence = E.fence()
    refence([wPB], fence)
    arB = mkar()
    E.dma("sync", lambda e: e.dma_start(
        out=wPh[:, :].rearrange("p (c o) -> p c o", c=NCD),
        in_=wgTh_sp.rearrange("c p o -> p c o")), rd=[wdT_spB], wr=[wPB])
    E.dma("sync", lambda e: e.dma_start(
        out=wPl[:, :].rearrange("p (c o) -> p c o", c=NCD),
        in_=wgTl_sp.rearrange("c p o -> p c o")), rd=[wdT_spB], wr=[wPB])
    alq2 = mkplace((WPB_B, WQB_B - 1024))
    drainB(phaseB(gate_sp, gate_spB, S_AM3, fence, alq2, arB), genWD(), 1, 1)

    global_absmax(1, [S_AM2, S_AM3])
    sc_ts(S_S2, S_AM2, 1.0 / QMAX, ALU.mult, 1e-12, ALU.max)
    sc_recip(S_IS2, S_S2)
    sc_ts(S_S3, S_AM3, 1.0 / QMAX, ALU.mult, 1e-12, ALU.max)
    sc_ts(S_S3H, S_S3, 0.5, ALU.mult)
    sc_recip(S_IS3, S_S3)

    fence = E.fence()
    refence([wQB], fence)
    alw = mkplace((0, WPB_B))
    arD = mkar()
    d_in = [alw([P, DW], F32) for _ in range(2)]
    d_inB = [Buf("d_in0", fence), Buf("d_in1", fence)]
    d_q3 = [alw([P, DW], F32) for _ in range(2)]
    d_q3B = [Buf("d_q30", fence), Buf("d_q31", fence)]
    d_sg = [alw([P, DW], F32) for _ in range(2)]
    d_sgB = [Buf("d_sg0", fence), Buf("d_sg1", fence)]
    d_g2 = [alw([P, DW], F32) for _ in range(2)]
    d_g2B = [Buf("d_g20", fence), Buf("d_g21", fence)]
    d_am = arD(2)
    d_amB, d_amtB = Buf("d_am", fence), Buf("d_amt", fence)
    f_g2 = [arD(DW) for _ in range(2)]
    f_g2B = [Buf("f_g20", fence), Buf("f_g21", fence)]
    f_up = [arD(DW) for _ in range(3)]
    f_upB = [Buf(f"f_up{i}", fence) for i in range(3)]
    f_q4 = [arD(DW) for _ in range(2)]
    f_q4B = [Buf("f_q40", fence), Buf("f_q41", fence)]
    f_q2 = [arD(DW) for _ in range(2)]
    f_q2B = [Buf("f_q20", fence), Buf("f_q21", fence)]
    f_am = arD(2)
    f_amB, f_amtB = Buf("f_am", fence), Buf("f_amt", fence)
    t_xnh = [alw([P, NCD * TA], BF16) for _ in range(2)]
    t_xnl = [alw([P, NCD * TA], BF16) for _ in range(2)]
    t_xnB = [Buf("t_xn0", fence), Buf("t_xn1", fence)]
    t_q = alw([P, NCD * TA], F32)
    t_qB = [Buf(f"t_q{c}", fence) for c in range(NCD)]
    t_qsq = [alw([P, TA], F32) for _ in range(2)]
    t_qsqB = [Buf("t_qsq0", fence), Buf("t_qsq1", fence)]
    t_nsq = alw([P, TA], F32)
    t_nsqB = Buf("t_nsq", fence)
    t_row = alw([1, H * TA], BF16)
    t_rowB = Buf("t_row", fence)
    t_qp = alw([P, 6 * TA], F32)
    t_qpB = [Buf(f"t_qp{i}", fence) for i in range(3)]
    t_num = [alw([P, TA], F32) for _ in range(2)]
    t_numB = [Buf("t_num0", fence), Buf("t_num1", fence)]
    t_den = alw([P, 4 * TA], F32)
    t_denB = [Buf("t_den0", fence), Buf("t_den1", fence)]
    t_nd = alw([P, NCD * TA], F32)
    t_ndh = alw([P, NCD * TA], BF16)
    t_ndl = alw([P, NCD * TA], BF16)
    t_ndB = [Buf(f"t_nd{c}", fence) for c in range(NCD)]
    t_att = [alw([P, D], F32) for _ in range(2)]
    t_attB = [Buf("t_att0", fence), Buf("t_att1", fence)]
    t_wst = [alw([P, D], F32) for _ in range(2)]
    t_wstB = [Buf("t_wst0", fence), Buf("t_wst1", fence)]

    def phaseD():
        E.op("vector", lambda e: e.memset(d_am[:, 0:1], 0.0), wr=[d_amB])
        steps = [(c, wti * DW) for c in range(NCI) for wti in range(NDW)]
        NS = len(steps)

        def d_dma(k):
            c, w0 = steps[k]
            ib = k % 2
            E.dma("sync", lambda e, c=c, w0=w0, ib=ib: e.dma_start(
                out=d_in[ib][:, :], in_=gate_sp[c, :, w0:w0 + DW]),
                rd=[gate_spB], wr=[d_inB[ib]])

        def d_light(k):
            ib = k % 2
            rne2_g(d_q3[ib][:, :], d_in[ib][:, :], scal(S_IS3),
                   [d_inB[ib], scB[S_IS3]], d_q3B[ib])
            clip_g(d_q3[ib][:, :], QMAX, -QMAX - 1.0, d_q3B[ib])

        def d_heavy(k):
            c, w0 = steps[k]
            ib = k % 2
            E.op("scalar", lambda e, ib=ib: e.activation(
                d_sg[ib][:, :], d_q3[ib][:, :], AF.Tanh, bias=0.0,
                scale=scal(S_S3H)),
                rd=[d_q3B[ib], scB[S_S3H]], wr=[d_sgB[ib]])
            E.op("vector", lambda e, ib=ib: e.tensor_scalar(
                d_sg[ib][:, :], d_sg[ib][:, :], 32768.0, 32768.0 + MAGIC,
                ALU.mult, ALU.add), rd=[], wr=[d_sgB[ib]])
            E.op("vector", lambda e, ib=ib: e.tensor_scalar(
                d_sg[ib][:, :], d_sg[ib][:, :], MAGIC, None,
                ALU.subtract), rd=[], wr=[d_sgB[ib]])
            clip_g(d_sg[ib][:, :], 65535.0, 0.0, d_sgB[ib])
            E.op("vector", lambda e, ib=ib: e.tensor_tensor(
                d_g2[ib][:, :], d_q3[ib][:, :], d_sg[ib][:, :],
                ALU.mult), rd=[d_q3B[ib], d_sgB[ib]], wr=[d_g2B[ib]])
            E.op("vector", lambda e, ib=ib: e.tensor_reduce(
                d_am[:, 1:2], d_g2[ib][:, :], AX.X, ALU.max,
                apply_absolute_value=True), rd=[d_g2B[ib]], wr=[d_amtB])
            E.op("vector", lambda e: e.tensor_tensor(
                d_am[:, 0:1], d_am[:, 0:1], d_am[:, 1:2], ALU.max),
                rd=[d_amtB], wr=[d_amB])
            E.dma("scalar", lambda e, c=c, w0=w0, ib=ib: e.dma_start(
                out=g2_sp[c, :, w0:w0 + DW], in_=d_g2[ib][:, :]),
                rd=[d_g2B[ib]], wr=[g2_spB])

        d_dma(0)
        for k in range(NS + 1):
            if k + 1 < NS:
                d_dma(k + 1)
            if k < NS:
                d_light(k)
            if k >= 1:
                d_heavy(k - 1)
            yield
        E.op("vector", lambda e: e.tensor_copy(scal(S_AM4), d_am[:, 0:1]),
             rd=[d_amB], wr=[scB[S_AM4]])

    def phaseF():
        E.op("vector", lambda e: e.memset(f_am[:, 0:1], 0.0), wr=[f_amB])
        steps = [(c, wti * DW) for c in range(NCI) for wti in range(NDW)]
        NS = len(steps)

        def f_dma(k):
            c, w0 = steps[k]
            ib = k % 2
            E.dma("sync", lambda e, c=c, w0=w0, ib=ib: e.dma_start(
                out=f_g2[ib][:, :], in_=g2_sp[c, :, w0:w0 + DW]),
                rd=[g2_spB], wr=[f_g2B[ib]])
            iu = k % 3
            E.dma("sync", lambda e, c=c, w0=w0, iu=iu: e.dma_start(
                out=f_up[iu][:, :], in_=up_sp[c, :, w0:w0 + DW]),
                rd=[up_spB], wr=[f_upB[iu]])

        def f_light(k):
            ib = k % 2
            rne2_g(f_q4[ib][:, :], f_g2[ib][:, :], scal(S_C4),
                   [f_g2B[ib], scB[S_C4]], f_q4B[ib])
            clip_g(f_q4[ib][:, :], QMAX, -QMAX - 1.0, f_q4B[ib])

        def f_heavy(k):
            c, w0 = steps[k]
            ib = k % 2
            iu = k % 3
            rne2("vector", f_q2[ib][:, :], f_up[iu][:, :], scal(S_IS2),
                 [f_upB[iu], scB[S_IS2]], f_q2B[ib])
            clip_g(f_q2[ib][:, :], QMAX, -QMAX - 1.0, f_q2B[ib])
            E.op("vector", lambda e, ib=ib: e.tensor_tensor(
                f_q4[ib][:, :], f_q4[ib][:, :], f_q2[ib][:, :], ALU.mult),
                rd=[f_q2B[ib]], wr=[f_q4B[ib]])
            E.op("vector", lambda e, ib=ib: e.tensor_reduce(
                f_am[:, 1:2], f_q4[ib][:, :], AX.X, ALU.max,
                apply_absolute_value=True), rd=[f_q4B[ib]], wr=[f_amtB])
            E.op("vector", lambda e: e.tensor_tensor(
                f_am[:, 0:1], f_am[:, 0:1], f_am[:, 1:2], ALU.max),
                rd=[f_amtB], wr=[f_amB])
            E.dma("scalar", lambda e, c=c, w0=w0, ib=ib: e.dma_start(
                out=o_sp[c, :, w0:w0 + DW], in_=f_q4[ib][:, :]),
                rd=[f_q4B[ib]], wr=[o_spB])

        f_dma(0)
        for k in range(NS + 1):
            if k + 1 < NS:
                f_dma(k + 1)
            if k < NS:
                f_light(k)
            if k >= 1:
                f_heavy(k - 1)
            yield
        E.op("vector", lambda e: e.tensor_copy(scal(S_AM5), f_am[:, 0:1]),
             rd=[f_amB], wr=[scB[S_AM5]])

    THROT = {}

    def phaseATT():
        for c in range(NCD):
            ws = c % 2
            E.dma("gpsimd", lambda e, c=c, ws=ws: e.dma_start(
                out=t_wst[ws][:, :], in_=Wqkv_d[c * P:(c + 1) * P, 0:D]),
                wr=[t_wstB[ws]])
            split_pair(t_wst[ws][:, :],
                       wQh[:, c * 2 * D:c * 2 * D + D],
                       wQl[:, c * 2 * D:c * 2 * D + D],
                       [t_wstB[ws]], wQB, wQB)
            E.dma("gpsimd", lambda e, c=c, ws=ws: e.dma_start(
                out=t_wst[ws][:, :], in_=Wout_d[c * P:(c + 1) * P, :]),
                wr=[t_wstB[ws]])
            split_pair(t_wst[ws][:, :],
                       wQh[:, c * 2 * D + D:(c + 1) * 2 * D],
                       wQl[:, c * 2 * D + D:(c + 1) * 2 * D],
                       [t_wstB[ws]], wQB, wQB)

        def wq_q(t, kc, c):
            return t[:, kc * 2 * D + c * P:kc * 2 * D + (c + 1) * P]

        def wq_o(t, kc, half):
            b = kc * 2 * D + D
            return t[:, b + half * 512:b + (half + 1) * 512]

        def att_load(it):
            t0 = it * TA
            xv = it % 2
            thr = [THROT["tok"]] if "tok" in THROT else []
            E.dma("gpsimd", lambda e, t0=t0, xv=xv: e.dma_start(
                out=t_xnh[xv][:, :].rearrange("p (c t) -> p c t", c=NCD),
                in_=xnh_sp[:, :, t0:t0 + TA].rearrange("c p t -> p c t")),
                rd=[xn_spB], wr=[t_xnB[xv]], extra=thr)
            E.dma("gpsimd", lambda e, t0=t0, xv=xv: e.dma_start(
                out=t_xnl[xv][:, :].rearrange("p (c t) -> p c t", c=NCD),
                in_=xnl_sp[:, :, t0:t0 + TA].rearrange("c p t -> p c t")),
                rd=[xn_spB], wr=[t_xnB[xv]])

        att_load(0)
        for it in range(NTA):
            t0 = it * TA
            xv = it % 2
            if it + 1 < NTA:
                att_load(it + 1)
            bkn = 2
            for c in range(NCD):
                bk = c % 2
                fns = []
                for kc in range(NCD):
                    fns += mm3(bk, wq_q(wQh, kc, c), wq_q(wQl, kc, c),
                               t_xnh[xv][:, kc * TA:(kc + 1) * TA],
                               t_xnl[xv][:, kc * TA:(kc + 1) * TA],
                               first=(kc == 0), last=(kc == NCD - 1),
                               n0=0, n1=TA)
                E.emit("tensor", fns, rd=[wQB, t_xnB[xv]], wr=[pbB[bk]])
                E.op("scalar", lambda e, c=c, bk=bk: e.activation(
                    t_q[:, c * TA:(c + 1) * TA], pbank[bk][:, 0:TA],
                    AF.Identity, bias=bq_sb[:, c:c + 1], scale=1.0),
                    rd=[pbB[bk], BQB], wr=[t_qB[c]])
                E.op("scalar", lambda e, c=c: e.activation(
                    t_qsq[c % 2][:, :], t_q[:, c * TA:(c + 1) * TA],
                    AF.Square, bias=0.0, scale=1.0),
                    rd=[t_qB[c]], wr=[t_qsqB[c % 2]])
                if c >= 1:
                    E.op("tensor", lambda e, c=c, bkn=bkn: e.matmul(
                        pbank[bkn][0:16, 0:TA],
                        nmask_sb[:, (c - 1) * 16:c * 16],
                        t_qsq[(c - 1) % 2][:, :], start=(c == 1),
                        stop=False),
                        rd=[t_qsqB[(c - 1) % 2], NMB], wr=[pbB[bkn]])
                if c % 2 == 1:
                    yield False
            E.op("tensor", lambda e, bkn=bkn: e.matmul(
                pbank[bkn][0:16, 0:TA],
                nmask_sb[:, (NCD - 1) * 16:NCD * 16],
                t_qsq[(NCD - 1) % 2][:, :], start=False, stop=True),
                rd=[t_qsqB[(NCD - 1) % 2], NMB], wr=[pbB[bkn]])
            E.op("scalar", lambda e, bkn=bkn: e.activation(
                t_nsq[0:16, 0:TA], pbank[bkn][0:16, 0:TA], AF.Exp,
                bias=ln6_ap(16), scale=-1.0), rd=[pbB[bkn], CSTB], wr=[t_nsqB])
            E.dma("gpsimd", lambda e: e.dma_start(
                out=t_row[0:1, :].rearrange("p (h t) -> p h t", h=H),
                in_=t_nsq[0:16, 0:TA]), rd=[t_nsqB], wr=[t_rowB])
            def K(h):
                c = h // 2
                rlo = (h % 2) * 64
                qs = (h % 3) * 2 * TA
                qpb = 2 + (h % 2)
                for mc in range(2):
                    E.op("tensor", lambda e, c=c, rlo=rlo, mc=mc, qpb=qpb:
                         e.matmul(pbank[qpb][:, mc * TA:(mc + 1) * TA],
                                  omg_sb[rlo:rlo + 64, mc * P:(mc + 1) * P],
                                  t_q[rlo:rlo + 64, c * TA:(c + 1) * TA],
                                  start=True, stop=True),
                         rd=[t_qB[c], OMB], wr=[pbB[qpb]])
                E.op("scalar", lambda e, qpb=qpb, qs=qs:
                     e.activation(
                         t_qp[:, qs:qs + 2 * TA],
                         pbank[qpb][:, 0:2 * TA], AF.Exp, bias=0.0,
                         scale=1.0),
                     rd=[pbB[qpb]], wr=[t_qpB[h % 3]])

            def ND(h):
                plo = (h % 2) * 64
                numbk = 4 + (h % 4) // 2
                dbk = 6 + (h % 4) // 2
                dn0 = (h % 2) * TA
                qs = (h % 3) * 2 * TA
                fns = [lambda e, h=h, mc=mc, plo=plo, numbk=numbk, qs=qs:
                       e.matmul(
                           pbank[numbk][plo:plo + 64, 0:TA],
                           kvl_sb[:, (h * 2 + mc) * 65:(h * 2 + mc) * 65 + 64],
                           t_qp[:, qs + mc * TA:qs + (mc + 1) * TA],
                           start=(mc == 0), stop=(mc == 1)) for mc in range(2)]
                E.emit("tensor", fns, rd=[KVLB, t_qpB[h % 3]],
                       wr=[pbB[numbk]])
                fns = [lambda e, h=h, mc=mc, dbk=dbk, dn0=dn0, qs=qs:
                       e.matmul(
                           pbank[dbk][0:1, dn0:dn0 + TA],
                           kvl_sb[:, (h * 2 + mc) * 65 + 64:
                                  (h * 2 + mc) * 65 + 65],
                           t_qp[:, qs + mc * TA:qs + (mc + 1) * TA],
                           start=(h % 2 == 0 and mc == 0),
                           stop=(h % 2 == 1 and mc == 1)) for mc in range(2)]
                E.emit("tensor", fns, rd=[KVLB, t_qpB[h % 3]], wr=[pbB[dbk]])
                if h % 2 == 1:
                    p = h // 2
                    dbk2 = 6 + (h % 4) // 2
                    ds = (p % 2) * 2 * TA
                    E.op("vector", lambda e, h=h, dbk2=dbk2, ds=ds:
                         e.tensor_tensor(
                             t_den[0:1, ds:ds + 2 * TA],
                             pbank[dbk2][0:1, 0:2 * TA],
                             t_row[0:1, (h - 1) * TA:(h + 1) * TA], ALU.add),
                         rd=[pbB[dbk2], t_rowB], wr=[t_denB[p % 2]])
                    E.op("vector", lambda e, ds=ds: e.reciprocal(
                        t_den[0:1, ds:ds + 2 * TA],
                        t_den[0:1, ds:ds + 2 * TA]),
                        rd=[], wr=[t_denB[p % 2]])

            def PAIR2(p):
                numbk = 4 + (p % 2)
                bbk = p % 2
                ds = (p % 2) * 2 * TA
                fns = [lambda e, sub=sub, ds=ds, bbk=bbk: e.matmul(
                    pbank[bbk][sub * 64:(sub + 1) * 64, 0:TA],
                    ones_sb[0:1, 0:64],
                    t_den[0:1, ds + sub * TA:ds + (sub + 1) * TA],
                    start=True, stop=True) for sub in range(2)]
                E.emit("tensor", fns, rd=[ONB, t_denB[p % 2]],
                       wr=[pbB[bbk]])
                E.op("scalar", lambda e, numbk=numbk, p=p: e.activation(
                    t_num[p % 2][:, :], pbank[numbk][:, 0:TA],
                    AF.Identity, bias=0.0, scale=1.0),
                    rd=[pbB[numbk]], wr=[t_numB[p % 2]])
                E.op("vector", lambda e, p=p, bbk=bbk: e.tensor_tensor(
                    t_nd[:, p * TA:(p + 1) * TA], t_num[p % 2][:, :],
                    pbank[bbk][:, 0:TA], ALU.mult),
                    rd=[t_numB[p % 2], pbB[bbk]], wr=[t_ndB[p]])
                split_pair(t_nd[:, p * TA:(p + 1) * TA],
                           t_ndh[:, p * TA:(p + 1) * TA],
                           t_ndl[:, p * TA:(p + 1) * TA],
                           [t_ndB[p]], t_ndB[p], t_ndB[p])

            for h in range(H + 3):
                if h < H:
                    K(h)
                if 2 <= h <= H + 1:
                    ND(h - 2)
                if h >= 4 and h % 2 == 0:
                    PAIR2((h - 4) // 2)
                if h >= 2:
                    yield (h % 2 == 1) or (h > H + 1)
            for s2 in range(2):
                ab = s2 % 2
                for half in range(2):
                    zbk = s2 * 2 + half
                    fns = []
                    for c2 in range(NCD):
                        fns += mm3(
                            zbk,
                            t_ndh[:, c2 * TA + s2 * P:
                                       c2 * TA + (s2 + 1) * P],
                            t_ndl[:, c2 * TA + s2 * P:
                                       c2 * TA + (s2 + 1) * P],
                            wq_o(wQh, c2, half), wq_o(wQl, c2, half),
                            first=(c2 == 0), last=False)
                    fns.append(lambda e, half=half, zbk=zbk: e.matmul(
                        pbank[zbk][:, :], ones_sb[0:1, 0:P],
                        bo_sb[0:1, half * 512:(half + 1) * 512],
                        start=False, stop=True))
                    E.emit("tensor", fns, rd=t_ndB + [wQB, ONB, BOB],
                           wr=[pbB[zbk]])
                    E.op("scalar", lambda e, ab=ab, half=half, zbk=zbk:
                         e.activation(
                             t_att[ab][:, half * 512:(half + 1) * 512],
                             pbank[zbk][:, :], AF.Identity, bias=0.0,
                             scale=1.0), rd=[pbB[zbk]], wr=[t_attB[ab]])
                E.dma("scalar", lambda e, t0=t0, s2=s2, ab=ab: e.dma_start(
                    out=att_sp[t0 + s2 * P:t0 + (s2 + 1) * P, :],
                    in_=t_att[ab][:, :]), rd=[t_attB[ab]], wr=[att_spB])
                if s2 == 0:
                    yield True
            yield True

    genD = phaseD()
    genATT = phaseATT()
    genF = phaseF()

    def drain2(g1, g2, r1, r2, throttle=False):
        done1 = done2 = False
        safe = True
        while not done1:
            for _ in range(r1):
                try:
                    next(g1)
                except StopIteration:
                    done1 = True
                    break
            if throttle:
                THROT["tok"] = ("V", E.cnt["V"])
            if not done2:
                for _ in range(r2):
                    try:
                        safe = next(g2)
                    except StopIteration:
                        done2 = True
                        safe = True
                        break
        while not safe and not done2:
            try:
                safe = next(g2)
            except StopIteration:
                done2 = True
                safe = True
        return done2

    att_done = drain2(genD, genATT, 1, 1)
    ATTS = {"done": att_done}

    def adv_att(n):
        for _ in range(n):
            if ATTS["done"]:
                return
            try:
                next(genATT)
            except StopIteration:
                ATTS["done"] = True

    global_absmax(2, [S_AM4], adv=adv_att)
    sc_ts(S_TMP1, S_S3, 1.0 / 65536.0, ALU.mult)
    sc_mul(S_S4, S_AM4, S_TMP1)
    sc_ts(S_S4, S_S4, 1.0 / QMAX, ALU.mult, 1e-12, ALU.max)
    sc_recip(S_C4, S_S4)
    sc_mul(S_C4, S_C4, S_TMP1)

    if not ATTS["done"]:
        ATTS["done"] = drain2(genF, genATT, 1, 1)
    else:
        for _ in genF:
            pass
    if not ATTS["done"]:
        for _ in genATT:
            pass
    global_absmax(3, [S_AM5])
    sc_mul(S_TMP1, S_S4, S_S2)
    sc_mul(S_S5, S_AM5, S_TMP1)
    sc_ts(S_S5, S_S5, 1.0 / QMAX, ALU.mult, 1e-12, ALU.max)
    sc_recip(S_C5, S_S5)
    sc_mul(S_C5, S_C5, S_TMP1)

    fence = E.fence()
    refence([wPB, wQB], fence)
    E.dma("sync", lambda e: e.dma_start(
        out=wPh[:, :].rearrange("p (c o) -> p c o", c=NCI),
        in_=wdTh_sp.rearrange("c p o -> p c o")), rd=[wdT_spB], wr=[wPB])
    E.dma("sync", lambda e: e.dma_start(
        out=wPl[:, :].rearrange("p (c o) -> p c o", c=NCI),
        in_=wdTl_sp.rearrange("c p o -> p c o")), rd=[wdT_spB], wr=[wPB])

    alg = mkplace((WPB_B, WQB_B))
    arG = mkar()
    g_o = [alg([P, NCI * P], F32) for _ in range(2)]
    g_oB = [Buf("g_o0", fence), Buf("g_o1", fence)]
    g_q5f = alg([P, NCI * P], F32)
    g_q5h = [alg([P, NCI * P], BF16) for _ in range(2)]
    g_q5l = [alg([P, NCI * P], BF16) for _ in range(2)]
    g_q5fB = Buf("g_q5f", fence)
    g_q5B = [Buf("g_q5a", fence), Buf("g_q5b", fence)]
    g_x = [arG(D) for _ in range(2)]
    g_xB = [Buf("g_x0", fence), Buf("g_x1", fence)]
    g_at = [arG(D) for _ in range(2)]
    g_atB = [Buf("g_at0", fence), Buf("g_at1", fence)]
    g_out = [arG(D) for _ in range(2)]
    g_outB = [Buf("g_out0", fence), Buf("g_out1", fence)]

    def g_chain(it):
        t0 = it * P
        ob = it % 2
        E.dma("sync", lambda e, t0=t0, ob=ob: e.dma_start(
            out=g_o[ob][:, :].rearrange("p (c t) -> p c t", c=NCI),
            in_=o_sp[:, :, t0:t0 + P].rearrange("c p t -> p c t")),
            rd=[o_spB], wr=[g_oB[ob]])
        E.dma("gpsimd", lambda e, t0=t0, ob=ob: e.dma_start(
            out=g_x[ob][:, :], in_=x_d[t0:t0 + P, :]), wr=[g_xB[ob]])
        E.dma("gpsimd", lambda e, t0=t0, ob=ob: e.dma_start(
            out=g_at[ob][:, :], in_=att_sp[t0:t0 + P, :]),
            rd=[att_spB], wr=[g_atB[ob]])
        rne2("vector", g_q5f[:, :], g_o[ob][:, :], scal(S_C5),
             [g_oB[ob], scB[S_C5]], g_q5fB)
        clip_g(g_q5f[:, :], QMAX, -QMAX - 1.0, g_q5fB)
        split_pair(g_q5f[:, :], g_q5h[ob][:, :], g_q5l[ob][:, :],
                   [g_q5fB], g_q5B[ob], g_q5B[ob])

    g_chain(0)
    for it in range(NTG):
        t0 = it * P
        ob = it % 2
        if it + 1 < NTG:
            g_chain(it + 1)
        for half in range(2):
            bk = (it % 2) * 2 + half
            fns = []
            for ic in range(NCI):
                fns += mm3(bk,
                           g_q5h[ob][:, ic * P:(ic + 1) * P],
                           g_q5l[ob][:, ic * P:(ic + 1) * P],
                           wPh[:, ic * D + half * 512:
                                    ic * D + (half + 1) * 512],
                           wPl[:, ic * D + half * 512:
                                    ic * D + (half + 1) * 512],
                           first=(ic == 0), last=(ic == NCI - 1))
            E.emit("tensor", fns, rd=[wPB, g_q5B[ob]], wr=[pbB[bk]])
            E.op("scalar", lambda e, ob=ob, half=half, bk=bk: e.activation(
                g_out[ob][:, half * 512:(half + 1) * 512], pbank[bk][:, :],
                AF.Identity, bias=0.0, scale=scal(S_S5)),
                rd=[pbB[bk], scB[S_S5]], wr=[g_outB[ob]])
        E.op("vector", lambda e, ob=ob: e.tensor_tensor(
            g_out[ob][:, :], g_out[ob][:, :], g_x[ob][:, :], ALU.add),
            rd=[g_xB[ob]], wr=[g_outB[ob]])
        E.op("vector", lambda e, ob=ob: e.tensor_tensor(
            g_out[ob][:, :], g_out[ob][:, :], g_at[ob][:, :], ALU.add),
            rd=[g_atB[ob]], wr=[g_outB[ob]])
        E.dma("scalar", lambda e, t0=t0, ob=ob: e.dma_start(
            out=out_d[t0:t0 + P, :], in_=g_out[ob][:, :]), rd=[g_outB[ob]])

    fin = [(f"AD{i}", E.cnt[f"AD{i}"]) for i in range(NSLOT)
           if E.cnt[f"AD{i}"] > 0]
    E.op("gpsimd", lambda e: e.nop(), extra=fin)

    sems = {s: st.enter_context(nc.semaphore(s)) for s in ALL_SEMS}
    block = st.enter_context(nc.Block())
    E.run(block, sems)
    st.close()
    return nc


_CACHE = {}


def _get_nc(T):
    if T not in _CACHE:
        _CACHE[T] = build_nc(T)
    return _CACHE[T]


def kernel(L0_attn_x, L0_attn_Wqkv, L0_attn_bqkv, L0_attn_Wout, L0_attn_bout,
           L0_attn_omega, L0_ffn_w_gate, L0_ffn_w_up, L0_ffn_w_down):
    x = np.ascontiguousarray(np.asarray(L0_attn_x, dtype=np.float32))
    B, T, Dm = x.shape
    assert Dm == D
    ident, nmask = _host_consts()
    nc = _get_nc(T)
    common = {
        "Wqkv": np.ascontiguousarray(L0_attn_Wqkv, dtype=np.float32),
        "bqkv": np.ascontiguousarray(L0_attn_bqkv, dtype=np.float32),
        "Wout": np.ascontiguousarray(L0_attn_Wout, dtype=np.float32),
        "bout": np.ascontiguousarray(L0_attn_bout, dtype=np.float32),
        "omega": np.ascontiguousarray(L0_attn_omega, dtype=np.float32),
        "w_gate": np.ascontiguousarray(L0_ffn_w_gate, dtype=np.float32),
        "w_up": np.ascontiguousarray(L0_ffn_w_up, dtype=np.float32),
        "w_down": np.ascontiguousarray(L0_ffn_w_down, dtype=np.float32),
        "ident": ident, "nmask": nmask,
    }
    in_maps = [dict(common, x=np.ascontiguousarray(x[i])) for i in range(B)]
    res = run_bass_kernel_spmd(nc, in_maps, core_ids=list(range(B)))
    return np.stack([np.asarray(res.results[i]["out"]) for i in range(B)],
                    axis=0)


# version 24 (final)
# speedup vs baseline: 1.0076x; 1.0076x over previous
import numpy as np
from contextlib import ExitStack

import concourse.bass as bass
import concourse.mybir as mybir
from concourse.bass_utils import run_bass_kernel_spmd

F32 = mybir.dt.float32
BF16 = mybir.dt.bfloat16
AF = mybir.ActivationFunctionType
ALU = mybir.AluOpType
AX = mybir.AxisListType

P = 128
D = 1024
H = 16
DH = 64
M = 256
I3 = 3072
NCD = D // P
NCI = I3 // P
QMAX = 32767.0
MAGIC = float(1.5 * 2 ** 23)
WQMAX = 7.0
N_CORES = 8

ENGS = ("sync", "tensor", "vector", "scalar", "gpsimd")
NSLOT = 8
DMA_QUEUES = ("SD", "AD", "GD")
ALL_SEMS = tuple(f"{q}{i}" for q in DMA_QUEUES for i in range(NSLOT)) + (
    "T", "V", "A", "G", "CC", "SY")


class Buf:
    __slots__ = ("w", "r", "init", "name", "multi", "wl")

    def __init__(self, name="", init=(), multi=False):
        self.w = None
        self.r = []
        self.init = tuple(init)
        self.name = name
        self.multi = multi
        self.wl = []


class Em:
    def __init__(self):
        self.q = {e: [] for e in ENGS}
        self.cnt = {s: 0 for s in ALL_SEMS}
        self.mark = {}
        self.esem = {"sync": "SY", "tensor": "T", "vector": "V",
                     "scalar": "A", "gpsimd": "G"}
        self.dq = {"sync": "SD", "scalar": "AD", "gpsimd": "GD"}
        self.rot = {q: 0 for q in DMA_QUEUES}

    def fence(self):
        return [(s, self.cnt[s]) for s in ALL_SEMS if self.cnt[s] > 0]

    def _waits(self, eng, toks):
        best = {}
        for t in toks:
            if t is None:
                continue
            s, v = t
            if v > best.get(s, -1):
                best[s] = v
        out = []
        for s, v in best.items():
            k = (eng, s)
            if v > self.mark.get(k, 0):
                self.mark[k] = v
                out.append((s, v))
        return out

    def emit(self, eng, fns, rd=(), wr=(), sem=None, inc=None, extra=()):
        if sem is None:
            sem = self.esem[eng]
        if inc is None:
            inc = 1
        toks = []
        for b in rd:
            toks.append(b.w)
            toks.extend(b.wl)
            toks.extend(b.init)
        for b in wr:
            toks.append(b.w)
            toks.extend(b.r)
            toks.extend(b.init)
            if not b.multi:
                toks.extend(b.wl)
        toks.extend(extra)
        waits = self._waits(eng, toks)
        self.cnt[sem] += inc
        tok = (sem, self.cnt[sem])
        self.q[eng].append((waits, list(fns), sem, inc))
        for b in rd:
            b.r.append(tok)
        for b in wr:
            if b.multi:
                b.wl.append(tok)
            else:
                b.w = tok
                b.r = []
                b.init = ()
        return tok

    def op(self, eng, fn, rd=(), wr=(), sem=None, inc=None, extra=()):
        return self.emit(eng, [fn], rd=rd, wr=wr, sem=sem, inc=inc, extra=extra)

    def dma(self, eng, fn, rd=(), wr=(), extra=()):
        q = self.dq[eng]
        sem = f"{q}{self.rot[q]}"
        self.rot[q] = (self.rot[q] + 1) % NSLOT
        ex = list(extra)
        if self.cnt[sem] > 0:
            ex.append((sem, self.cnt[sem]))
        return self.emit(eng, [fn], rd=rd, wr=wr, sem=sem, inc=16, extra=ex)

    def run(self, block, sems):
        def body(e, items):
            for waits, fns, sem, inc in items:
                for s, v in waits:
                    e.wait_ge(sems[s], v)
                ins = None
                for fn in fns:
                    ins = fn(e)
                if ins is not None and inc:
                    ins.then_inc(sems[sem], inc)
        for eng in ENGS:
            getattr(block, eng)(lambda e, it=tuple(self.q[eng]): body(e, it))


def _host_consts():
    ident = np.eye(P, dtype=np.float32)
    nmask = np.zeros((P, NCD * H), dtype=np.float32)
    for c in range(NCD):
        nmask[0:64, c * H + 2 * c] = -0.5
        nmask[64:128, c * H + 2 * c + 1] = -0.5
    return ident, nmask


def build_nc(T):
    assert T % 512 == 0
    NT = T // 512
    TA = 256
    NTA = T // TA
    NTG = T // P
    DW = min(512, T)
    NDW = T // DW

    nc = bass.Bass("TRN2", target_bir_lowering=False)

    x_d = nc.dram_tensor("x", [T, D], F32, kind="ExternalInput").ap()
    Wqkv_d = nc.dram_tensor("Wqkv", [D, 3 * D], F32, kind="ExternalInput").ap()
    bqkv_d = nc.dram_tensor("bqkv", [3 * D], F32, kind="ExternalInput").ap()
    Wout_d = nc.dram_tensor("Wout", [D, D], F32, kind="ExternalInput").ap()
    bout_d = nc.dram_tensor("bout", [D], F32, kind="ExternalInput").ap()
    omega_d = nc.dram_tensor("omega", [DH, M], F32, kind="ExternalInput").ap()
    wgate_d = nc.dram_tensor("w_gate", [I3, D], F32, kind="ExternalInput").ap()
    wup_d = nc.dram_tensor("w_up", [I3, D], F32, kind="ExternalInput").ap()
    wdown_d = nc.dram_tensor("w_down", [D, I3], F32, kind="ExternalInput").ap()
    ident_d = nc.dram_tensor("ident", [P, P], F32, kind="ExternalInput").ap()
    nmask_d = nc.dram_tensor("nmask", [P, NCD * H], F32, kind="ExternalInput").ap()
    out_d = nc.dram_tensor("out", [T, D], F32, kind="ExternalOutput").ap()

    xnh_sp = nc.dram_tensor("xnh_sp", [NCD, P, T], BF16).ap()
    xnl_sp = nc.dram_tensor("xnl_sp", [NCD, P, T], BF16).ap()
    up_sp = nc.dram_tensor("up_sp", [NCI, P, T], F32).ap()
    gate_sp = nc.dram_tensor("gate_sp", [NCI, P, T], F32).ap()
    g2_sp = nc.dram_tensor("g2_sp", [NCI, P, T], F32).ap()
    o_sp = nc.dram_tensor("o_sp", [NCI, P, T], F32).ap()
    att_sp = nc.dram_tensor("att_sp", [T, D], F32).ap()
    wdTh_sp = nc.dram_tensor("wdTh_sp", [NCI, P, D], BF16).ap()
    wdTl_sp = nc.dram_tensor("wdTl_sp", [NCI, P, D], BF16).ap()
    wgTh_sp = nc.dram_tensor("wgTh_sp", [NCD, P, I3], BF16).ap()
    wgTl_sp = nc.dram_tensor("wgTl_sp", [NCD, P, I3], BF16).ap()
    cc_in = [nc.dram_tensor(f"cc_in{i}", [1, 2], F32) for i in range(4)]
    cc_out = [nc.dram_tensor(f"cc_out{i}", [1, 2], F32) for i in range(4)]
    xn_spB = Buf("xn_sp", multi=True)
    up_spB = Buf("up_sp", multi=True)
    gate_spB = Buf("gate_sp", multi=True)
    g2_spB = Buf("g2_sp", multi=True)
    o_spB = Buf("o_sp", multi=True)
    att_spB = Buf("att_sp", multi=True)
    wdT_spB = Buf("wdT_sp", multi=True)

    st = ExitStack()

    def sbt(name, shape, dtype=F32):
        return st.enter_context(nc.sbuf_tensor(name, shape, dtype))

    ident_sb = sbt("ident_sb", [P, P])
    nmask_sb = sbt("nmask_sb", [P, NCD * H])
    omg_sb = sbt("omg_sb", [P, M])
    ones_sb = sbt("ones_sb", [1, P])
    bk_sb = sbt("bk_sb", [P, NCD])
    bq_sb = sbt("bq_sb", [P, NCD])
    bv_sb = sbt("bv_sb", [1, D])
    bo_sb = sbt("bo_sb", [1, D])
    consts_sb = sbt("consts_sb", [P, 4])
    kvl_sb = sbt("kvl_sb", [P, H * 2 * 65])
    SC_N = 20
    sc_sb = sbt("sc_sb", [P, SC_N])
    pack_sb = sbt("pack_sb", [1, 8])

    def scal(j):
        return sc_sb[:, j:j + 1]

    (S_AM1, S_AM2, S_AM3, S_AM4, S_AM5,
     S_S1, S_IS1, S_S2, S_IS2, S_S3, S_S3H, S_IS3,
     S_S4, S_C4, S_S5, S_C5, S_TMP1) = range(17)
    scB = [Buf(f"sc{j}") for j in range(SC_N)]
    packB = Buf("pack")

    WPB_B = 98304
    WQB_B = 65536
    BIG_B = WPB_B + WQB_B
    big_sb = sbt("big_sb", [P, BIG_B], mybir.dt.uint8)
    big_base = nc.lookup_mloc(big_sb).addr
    AR_EL = 7000
    ar_sb = sbt("ar_sb", [P, AR_EL])

    _uid = [0]

    def place(shape, dtype, off):
        _uid[0] += 1
        h = nc.virtual_sbuf_tensor(f"v{_uid[0]}", list(shape), dtype)
        nc._tensor(h.name, list(h.shape), h.dtype, type="SB")
        ml = nc.lookup_mloc(h)
        ml.allocated = True
        ml.addr = big_base + off
        return h.ap()

    def mkplace(*ranges):
        state = [[r[0], r[1]] for r in ranges]

        def alloc(shape, dtype):
            nb = int(np.prod(shape[1:])) * mybir.dt.size(dtype)
            nb = (nb + 31) // 32 * 32
            for s in state:
                if s[1] >= nb:
                    off = s[0]
                    s[0] += nb
                    s[1] -= nb
                    return place(shape, dtype, off)
            raise AssertionError(f"place overflow: need {nb}")
        return alloc

    def mkar():
        stt = [0]

        def alloc(nel):
            o = stt[0]
            stt[0] += nel
            assert stt[0] <= AR_EL, f"ar overflow {stt[0]}"
            return ar_sb[:, o:o + nel]
        return alloc

    wPh = place([P, NCI * D], BF16, 0)
    wPl = place([P, NCI * D], BF16, WPB_B // 2)
    wQh = place([P, NCD * 2 * D], BF16, WPB_B)
    wQl = place([P, NCD * 2 * D], BF16, WPB_B + WQB_B // 2)
    wPB = Buf("wP")
    wQB = Buf("wQ")

    def refence(bufs, f):
        for b in bufs:
            b.init = tuple(f)

    pbank = [st.enter_context(nc.psum_tensor(f"pb{i}", [P, 512], F32))
             for i in range(8)]
    pbB = [Buf(f"pb{i}") for i in range(8)]

    E = Em()
    IDB, NMB, OMB, ONB = Buf("id"), Buf("nm"), Buf("om"), Buf("on")
    BKB, BQB, BVB, BOB = Buf("bk"), Buf("bq"), Buf("bv"), Buf("bo")
    CSTB = Buf("cst")
    KVLB = Buf("kvl")

    E.dma("sync", lambda e: e.dma_start(out=ident_sb[:], in_=ident_d), wr=[IDB])
    E.dma("sync", lambda e: e.dma_start(out=nmask_sb[:], in_=nmask_d), wr=[NMB])
    E.dma("sync", lambda e: e.dma_start(out=omg_sb[0:64, :], in_=omega_d), wr=[OMB])
    E.dma("sync", lambda e: e.dma_start(out=omg_sb[64:128, :], in_=omega_d), wr=[OMB])
    bq3 = bqkv_d.rearrange("(s c p) -> s p c", s=3, c=NCD)
    E.dma("sync", lambda e: e.dma_start(out=bq_sb[:], in_=bq3[0],
                                        allow_slow_non_contiguous=True), wr=[BQB])
    E.dma("sync", lambda e: e.dma_start(out=bk_sb[:], in_=bq3[1],
                                        allow_slow_non_contiguous=True), wr=[BKB])
    E.dma("sync", lambda e: e.dma_start(out=bv_sb[0:1, :],
                                        in_=bqkv_d[2 * D:3 * D][None, :]), wr=[BVB])
    E.dma("sync", lambda e: e.dma_start(out=bo_sb[0:1, :], in_=bout_d[None, :]),
          wr=[BOB])
    E.op("vector", lambda e: e.memset(ones_sb[:], 1.0), wr=[ONB])
    E.op("vector", lambda e: e.memset(consts_sb[:, 0:1], 1e-5), wr=[CSTB])
    E.op("vector", lambda e: e.memset(consts_sb[:, 1:2], float(np.log(1e-6))),
         wr=[CSTB])
    E.op("vector", lambda e: e.memset(consts_sb[:, 2:3], MAGIC), wr=[CSTB])
    E.op("vector", lambda e: e.memset(consts_sb[:, 3:4], -MAGIC), wr=[CSTB])
    E.op("vector", lambda e: e.memset(pack_sb[:], 0.0), wr=[packB])
    eps_ap = lambda: consts_sb[:, 0:1]
    ln6_ap = lambda n=P: consts_sb[0:n, 1:2]
    mgp_ap = lambda: consts_sb[:, 2:3]
    mgn_ap = lambda: consts_sb[:, 3:4]

    def split_pair(src_ap, hi_ap, lo_ap, rd, hiB, loB, eng_hi="scalar"):
        if eng_hi == "scalar":
            E.op("scalar", lambda e: e.activation(hi_ap, src_ap, AF.Identity,
                                                  bias=0.0, scale=1.0),
                 rd=list(rd), wr=[hiB])
        else:
            E.op("vector", lambda e: e.tensor_copy(hi_ap, src_ap),
                 rd=list(rd), wr=[hiB])
        E.op("vector", lambda e: e.tensor_tensor(lo_ap, src_ap, hi_ap,
                                                 ALU.subtract),
             rd=list(rd) + [hiB], wr=[loB])

    def mm3(bank, lT_h, lT_l, r_h, r_l, first, last, n0=0, n1=512):
        return [
            lambda e: e.matmul(pbank[bank][:, n0:n1], lT_h, r_h,
                               start=first, stop=False),
            lambda e: e.matmul(pbank[bank][:, n0:n1], lT_l, r_h,
                               start=False, stop=False),
            lambda e: e.matmul(pbank[bank][:, n0:n1], lT_h, r_l,
                               start=False, stop=last),
        ]

    def sc_ts(dstj, srcj, s1, op0, s2=None, op1=None):
        E.op("vector", lambda e: e.tensor_scalar(
            scal(dstj), scal(srcj), s1, s2, op0,
            op1 if op1 is not None else ALU.bypass),
            rd=[scB[srcj]], wr=[scB[dstj]])

    def sc_recip(dstj, srcj):
        E.op("vector", lambda e: e.reciprocal(scal(dstj), scal(srcj)),
             rd=[scB[srcj]], wr=[scB[dstj]])

    def sc_mul(dstj, aj, bj):
        E.op("vector", lambda e: e.tensor_tensor(
            scal(dstj), scal(aj), scal(bj), ALU.mult),
            rd=[scB[aj], scB[bj]], wr=[scB[dstj]])

    def global_absmax(idx, slots, adv=None):
        if adv is not None:
            adv(3)
        for j, sj in enumerate(slots):
            E.op("tensor", lambda e, sj=sj: e.transpose(
                pbank[0][0:1, 0:P], scal(sj), ident_sb[:]),
                rd=[scB[sj], IDB], wr=[pbB[0]])
            E.op("vector", lambda e, j=j: e.tensor_reduce(
                pack_sb[0:1, j:j + 1], pbank[0][0:1, 0:P], AX.X, ALU.max,
                apply_absolute_value=True), rd=[pbB[0]], wr=[packB])
        gd_tok = E.dma("gpsimd", lambda e, idx=idx: e.dma_start(
            out=cc_in[idx].ap()[0:1, 0:2], in_=pack_sb[0:1, 0:2]), rd=[packB])
        E.op("gpsimd", lambda e, idx=idx: e.collective_compute(
            "AllReduce", ALU.max, replica_groups=[list(range(N_CORES))],
            ins=[cc_in[idx].ap().opt()], outs=[cc_out[idx].ap().opt()]),
            sem="CC", inc=1, extra=[gd_tok])
        cc_tok = ("CC", E.cnt["CC"])
        E.dma("gpsimd", lambda e, idx=idx: e.dma_start(
            out=pack_sb[0:1, 0:2], in_=cc_out[idx].ap()[0:1, 0:2]),
            wr=[packB], extra=[cc_tok])
        if adv is not None:
            adv(5)
        for j, sj in enumerate(slots):
            E.op("tensor", lambda e, j=j: e.matmul(
                pbank[0][:, 0:1], ones_sb[0:1, 0:P], pack_sb[0:1, j:j + 1],
                start=True, stop=True), rd=[packB, ONB], wr=[pbB[0]])
            E.op("vector", lambda e, sj=sj: e.tensor_copy(
                scal(sj), pbank[0][:, 0:1]), rd=[pbB[0]], wr=[scB[sj]])

    def rne2(eng, dst, src, sc1, rd, dstB):
        E.op(eng, lambda e: e.tensor_scalar(dst, src, sc1, MAGIC, ALU.mult,
                                            ALU.add), rd=rd, wr=[dstB])
        E.op(eng, lambda e: e.tensor_scalar(dst, dst, MAGIC, None,
                                            ALU.subtract), rd=[], wr=[dstB])

    def rne2_act(dst, src_, sc1_ap, rd, dstB):
        E.op("scalar", lambda e: e.activation(
            dst, src_, AF.Identity, bias=mgp_ap(), scale=sc1_ap),
            rd=list(rd) + [CSTB], wr=[dstB])
        E.op("scalar", lambda e: e.activation(
            dst, dst, AF.Identity, bias=mgn_ap(), scale=1.0),
            rd=[CSTB], wr=[dstB])

    def rne2_g(dst, src_, sc1_ap, rd, dstB):
        E.op("gpsimd", lambda e: e.tensor_scalar(dst, src_, sc1_ap, MAGIC,
                                                 ALU.mult, ALU.add),
             rd=list(rd), wr=[dstB])
        E.op("gpsimd", lambda e: e.tensor_scalar(dst, dst, 1.0, -MAGIC,
                                                 ALU.mult, ALU.add),
             rd=[], wr=[dstB])

    def clip_g(dst, lim_hi, lim_lo, dstB):
        E.op("gpsimd", lambda e: e.tensor_scalar(dst, dst, lim_hi, lim_lo,
                                                 ALU.min, ALU.max),
             rd=[], wr=[dstB])

    al = mkplace((0, WPB_B))
    arA = mkar()
    kvacc = al([P, H * M], F32)
    KVAB = Buf("kva")
    a_x = [al([P, D], F32) for _ in range(4)]
    a_xB = [Buf(f"a_x{i}") for i in range(4)]
    a_st = arA(16)
    a_stB = Buf("a_st")
    a_xn = al([P, NCD * 512], F32)
    a_xnB = [Buf(f"a_xn_{c}") for c in range(NCD)]
    a_xnh = al([P, NCD * 512], BF16)
    a_xnl = al([P, NCD * 512], BF16)
    a_xnpB = [Buf(f"a_xnp_{c}") for c in range(NCD)]
    a_k = al([P, NCD * 512], F32)
    a_kB = [Buf(f"a_k{c}") for c in range(NCD)]
    a_ksq = arA(NCD * 512)
    a_ksqB = [Buf(f"a_ksq{c}") for c in range(NCD)]
    a_nsq = arA(64)
    a_nsqB = [Buf(f"a_nsq{s}") for s in range(4)]
    a_kp = [arA(M) for _ in range(2)]
    a_kpB = [Buf("a_kp0"), Buf("a_kp1")]
    a_va = [al([P, H * 65], F32) for _ in range(3)] + [arA(H * 65)]
    a_vaB = [Buf(f"a_va{s}") for s in range(4)]
    a_am = arA(2)
    a_amB = Buf("a_am")
    a_amtB = Buf("a_amt")
    a_wst = None
    a_wstB = Buf("a_wst")

    a_wst = a_ksq[:, 0:2048]
    for c in range(NCD):
        E.dma("sync", lambda e, c=c: e.dma_start(
            out=a_wst[:, :], in_=Wqkv_d[c * P:(c + 1) * P, D:3 * D]),
            wr=[a_wstB])
        split_pair(a_wst[:, :], wQh[:, c * 2 * D:(c + 1) * 2 * D],
                   wQl[:, c * 2 * D:(c + 1) * 2 * D],
                   [a_wstB], wQB, wQB)
    for c in range(NCD):
        a_ksqB[c].init = (("V", E.cnt["V"]), ("A", E.cnt["A"]))

    def wQs(t, kc, which, j0, j1):
        base = kc * 2 * D + which * D
        return t[:, base + j0: base + j1]

    E.op("vector", lambda e: e.memset(kvacc[:, :], 0.0), wr=[KVAB])
    E.op("vector", lambda e: e.memset(a_am[:, 0:1], 0.0), wr=[a_amB])
    for s in range(4):
        E.op("vector", lambda e, s=s: e.memset(a_va[s][:, :], 1.0),
             wr=[a_vaB[s]])

    def a_ln(it):
        t0 = it * 512
        for s in range(4):
            tt = t0 + s * 128
            xb = s
            E.dma("sync", lambda e, tt=tt, xb=xb: e.dma_start(
                out=a_x[xb][:, :], in_=x_d[tt:tt + P, :]), wr=[a_xB[xb]])
            E.op("vector", lambda e, xb=xb: e.bn_stats(
                a_st[:, 0:6], a_x[xb][:, 0:512]),
                rd=[a_xB[xb]], wr=[a_stB])
            E.op("vector", lambda e, xb=xb: e.bn_stats(
                a_st[:, 6:12], a_x[xb][:, 512:1024]),
                rd=[a_xB[xb]], wr=[a_stB])
            E.op("vector", lambda e: e.bn_aggr(a_st[:, 12:14], a_st[:, 0:12]),
                 rd=[a_stB], wr=[a_stB])
            E.op("scalar", lambda e: e.activation(
                a_st[:, 14:15], a_st[:, 13:14], AF.Ln, bias=eps_ap(),
                scale=1.0), rd=[a_stB, CSTB], wr=[a_stB])
            E.op("scalar", lambda e: e.activation(
                a_st[:, 15:16], a_st[:, 14:15], AF.Exp, bias=0.0, scale=-0.5),
                rd=[a_stB], wr=[a_stB])
            E.op("vector", lambda e, xb=xb: e.tensor_scalar(
                a_x[xb][:, :], a_x[xb][:, :], a_st[:, 12:13],
                a_st[:, 15:16], ALU.subtract, ALU.mult),
                rd=[a_stB], wr=[a_xB[xb]])
            E.op("vector", lambda e, xb=xb: e.tensor_reduce(
                a_am[:, 1:2], a_x[xb][:, :], AX.X, ALU.max,
                apply_absolute_value=True), rd=[a_xB[xb]], wr=[a_amtB])
            E.op("vector", lambda e: e.tensor_tensor(
                a_am[:, 0:1], a_am[:, 0:1], a_am[:, 1:2], ALU.max),
                rd=[a_amtB], wr=[a_amB])

    def a_tr(it):
        t0 = it * 512
        for s in range(4):
            xb = s
            for c in range(NCD):
                bk = (s * NCD + c) % 2
                E.op("tensor", lambda e, xb=xb, c=c, bk=bk: e.transpose(
                    pbank[bk][:, 0:P], a_x[xb][:, c * P:(c + 1) * P],
                    ident_sb[:]), rd=[a_xB[xb], IDB], wr=[pbB[bk]])
                E.op("scalar", lambda e, c=c, s=s, bk=bk: e.activation(
                    a_xn[:, c * 512 + s * 128:c * 512 + (s + 1) * 128],
                    pbank[bk][:, 0:P], AF.Identity, bias=0.0, scale=1.0),
                    rd=[pbB[bk]], wr=[a_xnB[c]])
        for c in range(NCD):
            split_pair(a_xn[:, c * 512:(c + 1) * 512],
                       a_xnh[:, c * 512:(c + 1) * 512],
                       a_xnl[:, c * 512:(c + 1) * 512],
                       [a_xnB[c]], a_xnpB[c], a_xnpB[c])
        E.dma("scalar", lambda e, t0=t0: e.dma_start(
            out=xnh_sp[:, :, t0:t0 + 512].rearrange("c p t -> p c t"),
            in_=a_xnh[:, :].rearrange("p (c t) -> p c t", c=NCD)),
            rd=a_xnpB, wr=[xn_spB])
        E.dma("scalar", lambda e, t0=t0: e.dma_start(
            out=xnl_sp[:, :, t0:t0 + 512].rearrange("c p t -> p c t"),
            in_=a_xnl[:, :].rearrange("p (c t) -> p c t", c=NCD)),
            rd=a_xnpB, wr=[xn_spB])

    a_ln(0)
    a_tr(0)
    for it in range(NT):
        t0 = it * 512
        if it + 1 < NT:
            a_ln(it + 1)
        for c in range(NCD):
            bk = 2 + c % 2
            fns = []
            for kc in range(NCD):
                fns += mm3(bk, wQs(wQh, kc, 0, c * P, (c + 1) * P),
                           wQs(wQl, kc, 0, c * P, (c + 1) * P),
                           a_xnh[:, kc * 512:(kc + 1) * 512],
                           a_xnl[:, kc * 512:(kc + 1) * 512],
                           first=(kc == 0), last=(kc == NCD - 1))
            E.emit("tensor", fns, rd=[wQB] + a_xnpB, wr=[pbB[bk]])
            E.op("scalar", lambda e, c=c, bk=bk: e.activation(
                a_k[:, c * 512:(c + 1) * 512], pbank[bk][:, :],
                AF.Identity, bias=bk_sb[:, c:c + 1], scale=1.0),
                rd=[pbB[bk], BKB], wr=[a_kB[c]])
            E.op("vector", lambda e, c=c: e.tensor_tensor(
                a_ksq[:, c * 512:(c + 1) * 512],
                a_k[:, c * 512:(c + 1) * 512],
                a_k[:, c * 512:(c + 1) * 512], ALU.mult),
                rd=[a_kB[c]], wr=[a_ksqB[c]])
        for s in range(4):
            bk = 2 + s % 2
            fns = [lambda e, c=c, s=s, bk=bk: e.matmul(
                pbank[bk][:, 0:16],
                a_ksq[:, c * 512 + s * 128:c * 512 + (s + 1) * 128],
                nmask_sb[:, c * 16:(c + 1) * 16],
                start=(c == 0), stop=(c == NCD - 1)) for c in range(NCD)]
            E.emit("tensor", fns, rd=a_ksqB + [NMB], wr=[pbB[bk]])
            E.op("scalar", lambda e, s=s, bk=bk: e.activation(
                a_nsq[:, s * 16:(s + 1) * 16], pbank[bk][:, 0:16], AF.Identity,
                bias=0.0, scale=1.0), rd=[pbB[bk]], wr=[a_nsqB[s]])
        for s in range(4):
            for half in range(2):
                bk = 4 + (s * 2 + half) % 2
                fns = []
                for kc in range(NCD):
                    fns += mm3(
                        bk,
                        a_xnh[:, kc * 512 + s * 128:kc * 512 + (s + 1) * 128],
                        a_xnl[:, kc * 512 + s * 128:kc * 512 + (s + 1) * 128],
                        wQs(wQh, kc, 1, half * 512, (half + 1) * 512),
                        wQs(wQl, kc, 1, half * 512, (half + 1) * 512),
                        first=(kc == 0), last=False)
                fns.append(lambda e, half=half, bk=bk: e.matmul(
                    pbank[bk][:, :], ones_sb[0:1, 0:P],
                    bv_sb[0:1, half * 512:(half + 1) * 512],
                    start=False, stop=True))
                E.emit("tensor", fns, rd=[wQB, ONB, BVB] + a_xnpB,
                       wr=[pbB[bk]])
                E.op("scalar", lambda e, s=s, half=half, bk=bk: e.activation(
                    a_va[s][:, :].rearrange("p (h g) -> p h g", g=65)[
                        :, half * 8:(half + 1) * 8, 0:64],
                    pbank[bk][:, :].rearrange("p (h g) -> p h g", g=64),
                    AF.Identity, bias=0.0, scale=1.0),
                    rd=[pbB[bk]], wr=[a_vaB[s]])
        if it + 1 < NT:
            a_tr(it + 1)
        def a_km(h, s):
            c = h // 2
            rlo = (h % 2) * 64
            kb = s % 2
            E.op("tensor", lambda e, c=c, rlo=rlo, s=s, kb=kb: e.matmul(
                pbank[kb][:, 0:M],
                a_k[rlo:rlo + 64, c * 512 + s * 128:c * 512 + (s + 1) * 128],
                omg_sb[rlo:rlo + 64, :], start=True, stop=True),
                rd=[a_kB[c], OMB], wr=[pbB[kb]])
            E.op("scalar", lambda e, h=h, s=s, kb=kb: e.activation(
                a_kp[kb][:, :], pbank[kb][:, 0:M], AF.Exp,
                bias=a_nsq[:, s * 16 + h:s * 16 + h + 1], scale=1.0),
                rd=[pbB[kb], a_nsqB[s]], wr=[a_kpB[kb]])

        def a_kv(h, s):
            kb = s % 2
            kvbk = 6 + h % 2
            E.op("tensor", lambda e, h=h, s=s, kb=kb, kvbk=kvbk: e.matmul(
                pbank[kvbk][0:65, 0:M],
                a_va[s][:, h * 65:(h + 1) * 65],
                a_kp[kb][:, :], start=(s == 0), stop=(s == 3)),
                rd=[a_vaB[s], a_kpB[kb]], wr=[pbB[kvbk]])
            if s == 3:
                E.op("vector", lambda e, h=h, kvbk=kvbk: e.tensor_tensor(
                    kvacc[0:65, h * M:(h + 1) * M],
                    kvacc[0:65, h * M:(h + 1) * M],
                    pbank[kvbk][0:65, 0:M], ALU.add),
                    rd=[pbB[kvbk]], wr=[KVAB])

        prev = None
        for h in range(H):
            for s in range(4):
                a_km(h, s)
                if prev is not None:
                    a_kv(*prev)
                prev = (h, s)
        a_kv(*prev)

    for h in range(H):
        for mc in range(2):
            bk = (h * 2 + mc) % 2
            E.op("tensor", lambda e, h=h, mc=mc, bk=bk: e.transpose(
                pbank[bk][:, 0:65],
                kvacc[0:65, h * M + mc * P:h * M + mc * P + P],
                ident_sb[0:65, 0:65]), rd=[KVAB, IDB], wr=[pbB[bk]])
            E.op("scalar", lambda e, h=h, mc=mc, bk=bk: e.activation(
                kvl_sb[:, (h * 2 + mc) * 65:(h * 2 + mc + 1) * 65],
                pbank[bk][:, 0:65], AF.Identity, bias=0.0, scale=1.0),
                rd=[pbB[bk]], wr=[KVLB])

    E.op("vector", lambda e: e.tensor_copy(scal(S_AM1), a_am[:, 0:1]),
         rd=[a_amB], wr=[scB[S_AM1]])
    global_absmax(0, [S_AM1])
    sc_ts(S_S1, S_AM1, 1.0 / QMAX, ALU.mult, 1e-12, ALU.max)
    sc_recip(S_IS1, S_S1)

    def dequant_transpose(w_ap, n_oc, n_k, dsth, dstl, stages, hbm=None,
                          ev_st=None):
        NBK = n_k // 32

        def chain(oc):
            stg, stgB = stages[oc % len(stages)]
            E.dma("sync", lambda e, oc=oc, stg=stg: e.dma_start(
                out=stg[:, 0:n_k], in_=w_ap[oc * P:(oc + 1) * P, :]), wr=[stgB])
            E.op("vector", lambda e, stg=stg: e.tensor_reduce(
                stg[:, n_k:n_k + NBK],
                stg[:, 0:n_k].rearrange("p (nb b) -> p nb b", b=32),
                AX.X, ALU.max, apply_absolute_value=True), rd=[], wr=[stgB])
            sA = stg[:, n_k:n_k + NBK]
            sR = stg[:, n_k + NBK:n_k + 2 * NBK]
            sS = stg[:, n_k + 2 * NBK:n_k + 3 * NBK]
            E.op("vector", lambda e, sA=sA, sS=sS: e.tensor_scalar(
                sS, sA, 1.0 / 7.0, None, ALU.mult), rd=[], wr=[stgB])
            E.op("vector", lambda e, sR=sR, sS=sS: e.tensor_scalar(
                sR, sS, -7.0, None, ALU.mult), rd=[], wr=[stgB])
            E.op("vector", lambda e, sA=sA, sR=sR: e.tensor_tensor(
                sR, sA, sR, ALU.add), rd=[], wr=[stgB])
            E.op("vector", lambda e, sR=sR: e.tensor_scalar(
                sR, sR, 1.0 / 7.0, None, ALU.mult), rd=[], wr=[stgB])
            E.op("vector", lambda e, sA=sA, sR=sR, sS=sS: e.tensor_tensor(
                sA, sS, sR, ALU.add), rd=[], wr=[stgB])
            E.op("vector", lambda e, sA=sA: e.tensor_scalar(
                sA, sA, 1e-12, None, ALU.max), rd=[], wr=[stgB])
            E.op("vector", lambda e, sA=sA, sR=sR: e.reciprocal(sR, sA),
                 rd=[], wr=[stgB])
            E.op("gpsimd", lambda e, stg=stg, sR=sR: e.tensor_tensor(
                stg[:, 0:n_k].rearrange("p (nb b) -> p nb b", b=32),
                stg[:, 0:n_k].rearrange("p (nb b) -> p nb b", b=32),
                sR[:, :, None].broadcast_to((P, NBK, 32)), ALU.mult),
                rd=[], wr=[stgB])
            E.op("gpsimd", lambda e, stg=stg: e.tensor_scalar(
                stg[:, 0:n_k], stg[:, 0:n_k], 1.0, MAGIC,
                ALU.mult, ALU.add), rd=[], wr=[stgB])
            E.op("gpsimd", lambda e, stg=stg: e.tensor_scalar(
                stg[:, 0:n_k], stg[:, 0:n_k], 1.0, -MAGIC,
                ALU.mult, ALU.add), rd=[], wr=[stgB])
            E.op("gpsimd", lambda e, stg=stg: e.tensor_scalar(
                stg[:, 0:n_k], stg[:, 0:n_k], WQMAX, -WQMAX - 1.0,
                ALU.min, ALU.max), rd=[], wr=[stgB])
            E.op("gpsimd", lambda e, stg=stg, sA=sA: e.tensor_tensor(
                stg[:, 0:n_k].rearrange("p (nb b) -> p nb b", b=32),
                stg[:, 0:n_k].rearrange("p (nb b) -> p nb b", b=32),
                sA[:, :, None].broadcast_to((P, NBK, 32)), ALU.mult),
                rd=[], wr=[stgB])

        def xpose(oc):
            stg, stgB = stages[oc % len(stages)]
            for kc in range(n_k // P):
                bk = 4 + kc % 4
                E.op("tensor", lambda e, stg=stg, kc=kc, bk=bk: e.transpose(
                    pbank[bk][:, 0:P], stg[:, kc * P:(kc + 1) * P],
                    ident_sb[:]), rd=[stgB, IDB], wr=[pbB[bk]])
                if hbm is None:
                    E.op("scalar", lambda e, kc=kc, oc=oc, bk=bk:
                         e.activation(dsth(kc, oc), pbank[bk][:, 0:P],
                                      AF.Identity, bias=0.0, scale=1.0),
                         rd=[pbB[bk]], wr=[wPB])
                    E.op("vector", lambda e, kc=kc, oc=oc, bk=bk:
                         e.tensor_tensor(dstl(kc, oc), pbank[bk][:, 0:P],
                                         dsth(kc, oc), ALU.subtract),
                         rd=[pbB[bk]], wr=[wPB])
                else:
                    evh, evl, evB = ev_st[kc % 2]
                    E.op("scalar", lambda e, evh=evh, bk=bk: e.activation(
                        evh[:, 0:P], pbank[bk][:, 0:P], AF.Identity,
                        bias=0.0, scale=1.0), rd=[pbB[bk]], wr=[evB])
                    E.op("vector", lambda e, evh=evh, evl=evl, bk=bk:
                         e.tensor_tensor(evl[:, 0:P], pbank[bk][:, 0:P],
                                         evh[:, 0:P], ALU.subtract),
                         rd=[pbB[bk]], wr=[evB])
                    hh, hl = hbm
                    E.dma("gpsimd", lambda e, evh=evh, kc=kc, oc=oc, hh=hh:
                          e.dma_start(out=hh[kc, :, oc * P:(oc + 1) * P],
                                      in_=evh[:, 0:P]),
                          rd=[evB], wr=[wdT_spB])
                    E.dma("gpsimd", lambda e, evl=evl, kc=kc, oc=oc, hl=hl:
                          e.dma_start(out=hl[kc, :, oc * P:(oc + 1) * P],
                                      in_=evl[:, 0:P]),
                          rd=[evB], wr=[wdT_spB])

        chain(0)
        for oc in range(1, n_oc + 1):
            if oc < n_oc:
                chain(oc)
            xpose(oc - 1)
            yield

    def wPu_h(kc, oc):
        return wPh[:, kc * I3 + oc * P:kc * I3 + (oc + 1) * P]

    def wPu_l(kc, oc):
        return wPl[:, kc * I3 + oc * P:kc * I3 + (oc + 1) * P]

    def phaseB(w_sp_dst, w_sp_dstB, amj, fence, alq, arB_):
        b_xnh = [alq([P, NCD * 512], BF16)]
        b_xnl = [alq([P, NCD * 512], BF16)]
        b_xnB = [Buf("b_xn0", fence)]
        b_q1h = [alq([P, NCD * 512], BF16) for _ in range(2)]
        b_q1l = [alq([P, NCD * 512], BF16) for _ in range(2)]
        b_q1B = [[Buf(f"b_q1{q}_{c}", fence) for c in range(NCD)]
                 for q in range(2)]
        b_f = alq([P, 2 * 512], F32)
        b_fB = [Buf("b_f0", fence), Buf("b_f1", fence)]
        b_ev = [alq([P, 2 * 512], F32) for _ in range(2)]
        b_evB = [[Buf(f"b_ev{g}_{j}", fence) for j in range(2)]
                 for g in range(2)]
        b_am = arB_(2)
        b_amB, b_amtB = Buf("b_am", fence), Buf("b_amt", fence)
        E.op("vector", lambda e: e.memset(b_am[:, 0:1], 0.0), wr=[b_amB])

        def b_chain(it):
            t0 = it * 512
            qb = it % 2
            E.dma("sync", lambda e, t0=t0: e.dma_start(
                out=b_xnh[0][:, :].rearrange("p (c t) -> p c t", c=NCD),
                in_=xnh_sp[:, :, t0:t0 + 512].rearrange("c p t -> p c t")),
                rd=[xn_spB], wr=[b_xnB[0]])
            E.dma("sync", lambda e, t0=t0: e.dma_start(
                out=b_xnl[0][:, :].rearrange("p (c t) -> p c t", c=NCD),
                in_=xnl_sp[:, :, t0:t0 + 512].rearrange("c p t -> p c t")),
                rd=[xn_spB], wr=[b_xnB[0]])
            for c in range(NCD):
                fs = c % 2
                E.op("vector", lambda e, c=c, fs=fs: e.tensor_tensor(
                    b_f[:, fs * 512:(fs + 1) * 512],
                    b_xnh[0][:, c * 512:(c + 1) * 512],
                    b_xnl[0][:, c * 512:(c + 1) * 512], ALU.add),
                    rd=[b_xnB[0]], wr=[b_fB[fs]])
                rne2("vector", b_f[:, fs * 512:(fs + 1) * 512],
                     b_f[:, fs * 512:(fs + 1) * 512], scal(S_IS1),
                     [scB[S_IS1]], b_fB[fs])
                clip_g(b_f[:, fs * 512:(fs + 1) * 512], QMAX, -QMAX - 1.0,
                       b_fB[fs])
                split_pair(b_f[:, fs * 512:(fs + 1) * 512],
                           b_q1h[qb][:, c * 512:(c + 1) * 512],
                           b_q1l[qb][:, c * 512:(c + 1) * 512],
                           [b_fB[fs]], b_q1B[qb][c], b_q1B[qb][c])

        b_chain(0)
        for it in range(NT):
            t0 = it * 512
            qb = it % 2
            if it + 1 < NT:
                b_chain(it + 1)
            for ic in range(NCI):
                bk = ic % 4
                g = (ic // 2) % 2
                j = ic % 2
                fns = []
                for kc in range(NCD):
                    fns += mm3(bk, wPu_h(kc, ic), wPu_l(kc, ic),
                               b_q1h[qb][:, kc * 512:(kc + 1) * 512],
                               b_q1l[qb][:, kc * 512:(kc + 1) * 512],
                               first=(kc == 0), last=(kc == NCD - 1))
                E.emit("tensor", fns, rd=[wPB] + b_q1B[qb], wr=[pbB[bk]])
                E.op("scalar", lambda e, bk=bk, g=g, j=j: e.activation(
                    b_ev[g][:, j * 512:(j + 1) * 512], pbank[bk][:, :],
                    AF.Identity, bias=0.0, scale=scal(S_S1)),
                    rd=[pbB[bk], scB[S_S1]], wr=[b_evB[g][j]])
                E.op("vector", lambda e, g=g, j=j: e.tensor_reduce(
                    b_am[:, 1:2], b_ev[g][:, j * 512:(j + 1) * 512], AX.X,
                    ALU.max, apply_absolute_value=True),
                    rd=[b_evB[g][j]], wr=[b_amtB])
                E.op("vector", lambda e: e.tensor_tensor(
                    b_am[:, 0:1], b_am[:, 0:1], b_am[:, 1:2], ALU.max),
                    rd=[b_amtB], wr=[b_amB])
                if j == 1:
                    c0 = ic - 1
                    E.dma("scalar", lambda e, c0=c0, t0=t0, g=g: e.dma_start(
                        out=w_sp_dst[c0:c0 + 2, :, t0:t0 + 512].rearrange(
                            "c p t -> p c t"),
                        in_=b_ev[g][:, :].rearrange("p (c t) -> p c t", c=2)),
                        rd=b_evB[g], wr=[w_sp_dstB])
                if ic == 11:
                    yield
            yield
        E.op("vector", lambda e: e.tensor_copy(scal(amj), b_am[:, 0:1]),
             rd=[b_amB], wr=[scB[amj]])

    fence = E.fence()
    refence([wPB], fence)
    arB = mkar()
    dq_st = [(arB(1024 + 96), Buf("dqA", fence)),
             (arB(1024 + 96), Buf("dqB", fence))]
    for _ in dequant_transpose(wup_d, NCI, D, wPu_h, wPu_l, dq_st):
        pass
    dq_std = [(arB(1536 + 144), Buf("dqD0", fence)),
              (arB(1536 + 144), Buf("dqD1", fence))]
    dq_ev = [(place([P, P], BF16, WPB_B + WQB_B - 1024),
              place([P, P], BF16, WPB_B + WQB_B - 768), Buf("dqE0", fence)),
             (place([P, P], BF16, WPB_B + WQB_B - 512),
              place([P, P], BF16, WPB_B + WQB_B - 256), Buf("dqE1", fence))]

    def genWG():
        yield from dequant_transpose(
            wgate_d, NCI, D, None, None, dq_st,
            hbm=(wgTh_sp, wgTl_sp), ev_st=dq_ev)

    def genWD():
        for half in range(2):
            yield from dequant_transpose(
                wdown_d[:, half * 1536:(half + 1) * 1536], NCD, 1536,
                None, None, dq_std,
                hbm=(wdTh_sp[half * 12:(half + 1) * 12],
                     wdTl_sp[half * 12:(half + 1) * 12]), ev_st=dq_ev)

    def drainB(g1, g2, r1, r2):
        done2 = False
        while True:
            try:
                for _ in range(r1):
                    next(g1)
            except StopIteration:
                break
            if not done2:
                try:
                    for _ in range(r2):
                        next(g2)
                except StopIteration:
                    done2 = True
        if not done2:
            for _ in g2:
                pass

    alq2 = mkplace((WPB_B, WQB_B - 1024))
    drainB(phaseB(up_sp, up_spB, S_AM2, fence, alq2, arB), genWG(), 1, 1)

    fence = E.fence()
    refence([wPB], fence)
    arB = mkar()
    E.dma("sync", lambda e: e.dma_start(
        out=wPh[:, :].rearrange("p (c o) -> p c o", c=NCD),
        in_=wgTh_sp.rearrange("c p o -> p c o")), rd=[wdT_spB], wr=[wPB])
    E.dma("sync", lambda e: e.dma_start(
        out=wPl[:, :].rearrange("p (c o) -> p c o", c=NCD),
        in_=wgTl_sp.rearrange("c p o -> p c o")), rd=[wdT_spB], wr=[wPB])
    alq2 = mkplace((WPB_B, WQB_B - 1024))
    drainB(phaseB(gate_sp, gate_spB, S_AM3, fence, alq2, arB), genWD(), 1, 1)

    global_absmax(1, [S_AM2, S_AM3])
    sc_ts(S_S2, S_AM2, 1.0 / QMAX, ALU.mult, 1e-12, ALU.max)
    sc_recip(S_IS2, S_S2)
    sc_ts(S_S3, S_AM3, 1.0 / QMAX, ALU.mult, 1e-12, ALU.max)
    sc_ts(S_S3H, S_S3, 0.5, ALU.mult)
    sc_recip(S_IS3, S_S3)

    fence = E.fence()
    refence([wQB], fence)
    alw = mkplace((0, WPB_B))
    arD = mkar()
    d_in = [alw([P, DW], F32) for _ in range(2)]
    d_inB = [Buf("d_in0", fence), Buf("d_in1", fence)]
    d_q3 = [alw([P, DW], F32) for _ in range(2)]
    d_q3B = [Buf("d_q30", fence), Buf("d_q31", fence)]
    d_sg = [alw([P, DW], F32) for _ in range(2)]
    d_sgB = [Buf("d_sg0", fence), Buf("d_sg1", fence)]
    d_g2 = [alw([P, DW], F32) for _ in range(2)]
    d_g2B = [Buf("d_g20", fence), Buf("d_g21", fence)]
    d_am = arD(2)
    d_amB, d_amtB = Buf("d_am", fence), Buf("d_amt", fence)
    f_g2 = [arD(DW) for _ in range(2)]
    f_g2B = [Buf("f_g20", fence), Buf("f_g21", fence)]
    f_up = [arD(DW) for _ in range(3)]
    f_upB = [Buf(f"f_up{i}", fence) for i in range(3)]
    f_q4 = [arD(DW) for _ in range(2)]
    f_q4B = [Buf("f_q40", fence), Buf("f_q41", fence)]
    f_q2 = [arD(DW) for _ in range(2)]
    f_q2B = [Buf("f_q20", fence), Buf("f_q21", fence)]
    f_am = arD(2)
    f_amB, f_amtB = Buf("f_am", fence), Buf("f_amt", fence)
    t_xnh = [alw([P, NCD * TA], BF16) for _ in range(2)]
    t_xnl = [alw([P, NCD * TA], BF16) for _ in range(2)]
    t_xnB = [Buf("t_xn0", fence), Buf("t_xn1", fence)]
    t_q = alw([P, NCD * TA], F32)
    t_qB = [Buf(f"t_q{c}", fence) for c in range(NCD)]
    t_qsq = [alw([P, TA], F32) for _ in range(2)]
    t_qsqB = [Buf("t_qsq0", fence), Buf("t_qsq1", fence)]
    t_nsq = alw([P, TA], F32)
    t_nsqB = Buf("t_nsq", fence)
    t_row = alw([1, H * TA], BF16)
    t_rowB = Buf("t_row", fence)
    t_qp = alw([P, 6 * TA], F32)
    t_qpB = [Buf(f"t_qp{i}", fence) for i in range(3)]
    t_num = [alw([P, TA], F32) for _ in range(2)]
    t_numB = [Buf("t_num0", fence), Buf("t_num1", fence)]
    t_den = alw([P, 4 * TA], F32)
    t_denB = [Buf("t_den0", fence), Buf("t_den1", fence)]
    t_nd = alw([P, NCD * TA], F32)
    t_ndh = alw([P, NCD * TA], BF16)
    t_ndl = alw([P, NCD * TA], BF16)
    t_ndB = [Buf(f"t_nd{c}", fence) for c in range(NCD)]
    t_att = [alw([P, D], F32) for _ in range(2)]
    t_attB = [Buf("t_att0", fence), Buf("t_att1", fence)]
    t_wst = [alw([P, D], F32) for _ in range(2)]
    t_wstB = [Buf("t_wst0", fence), Buf("t_wst1", fence)]

    def phaseD():
        E.op("vector", lambda e: e.memset(d_am[:, 0:1], 0.0), wr=[d_amB])
        steps = [(c, wti * DW) for c in range(NCI) for wti in range(NDW)]
        NS = len(steps)

        def d_dma(k):
            c, w0 = steps[k]
            ib = k % 2
            E.dma("sync", lambda e, c=c, w0=w0, ib=ib: e.dma_start(
                out=d_in[ib][:, :], in_=gate_sp[c, :, w0:w0 + DW]),
                rd=[gate_spB], wr=[d_inB[ib]])

        def d_light(k):
            ib = k % 2
            rne2_g(d_q3[ib][:, :], d_in[ib][:, :], scal(S_IS3),
                   [d_inB[ib], scB[S_IS3]], d_q3B[ib])
            clip_g(d_q3[ib][:, :], QMAX, -QMAX - 1.0, d_q3B[ib])

        def d_heavy(k):
            c, w0 = steps[k]
            ib = k % 2
            E.op("scalar", lambda e, ib=ib: e.activation(
                d_sg[ib][:, :], d_q3[ib][:, :], AF.Tanh, bias=0.0,
                scale=scal(S_S3H)),
                rd=[d_q3B[ib], scB[S_S3H]], wr=[d_sgB[ib]])
            E.op("vector", lambda e, ib=ib: e.tensor_scalar(
                d_sg[ib][:, :], d_sg[ib][:, :], 32768.0, 32768.0 + MAGIC,
                ALU.mult, ALU.add), rd=[], wr=[d_sgB[ib]])
            E.op("vector", lambda e, ib=ib: e.tensor_scalar(
                d_sg[ib][:, :], d_sg[ib][:, :], MAGIC, None,
                ALU.subtract), rd=[], wr=[d_sgB[ib]])
            clip_g(d_sg[ib][:, :], 65535.0, 0.0, d_sgB[ib])
            E.op("vector", lambda e, ib=ib: e.tensor_tensor(
                d_g2[ib][:, :], d_q3[ib][:, :], d_sg[ib][:, :],
                ALU.mult), rd=[d_q3B[ib], d_sgB[ib]], wr=[d_g2B[ib]])
            E.op("vector", lambda e, ib=ib: e.tensor_reduce(
                d_am[:, 1:2], d_g2[ib][:, :], AX.X, ALU.max,
                apply_absolute_value=True), rd=[d_g2B[ib]], wr=[d_amtB])
            E.op("vector", lambda e: e.tensor_tensor(
                d_am[:, 0:1], d_am[:, 0:1], d_am[:, 1:2], ALU.max),
                rd=[d_amtB], wr=[d_amB])
            E.dma("scalar", lambda e, c=c, w0=w0, ib=ib: e.dma_start(
                out=g2_sp[c, :, w0:w0 + DW], in_=d_g2[ib][:, :]),
                rd=[d_g2B[ib]], wr=[g2_spB])

        d_dma(0)
        for k in range(NS + 1):
            if k + 1 < NS:
                d_dma(k + 1)
            if k < NS:
                d_light(k)
            if k >= 1:
                d_heavy(k - 1)
            yield
        E.op("vector", lambda e: e.tensor_copy(scal(S_AM4), d_am[:, 0:1]),
             rd=[d_amB], wr=[scB[S_AM4]])

    def phaseF():
        E.op("vector", lambda e: e.memset(f_am[:, 0:1], 0.0), wr=[f_amB])
        steps = [(c, wti * DW) for c in range(NCI) for wti in range(NDW)]
        NS = len(steps)

        def f_dma(k):
            c, w0 = steps[k]
            ib = k % 2
            E.dma("sync", lambda e, c=c, w0=w0, ib=ib: e.dma_start(
                out=f_g2[ib][:, :], in_=g2_sp[c, :, w0:w0 + DW]),
                rd=[g2_spB], wr=[f_g2B[ib]])
            iu = k % 3
            E.dma("sync", lambda e, c=c, w0=w0, iu=iu: e.dma_start(
                out=f_up[iu][:, :], in_=up_sp[c, :, w0:w0 + DW]),
                rd=[up_spB], wr=[f_upB[iu]])

        def f_light(k):
            ib = k % 2
            rne2_g(f_q4[ib][:, :], f_g2[ib][:, :], scal(S_C4),
                   [f_g2B[ib], scB[S_C4]], f_q4B[ib])
            clip_g(f_q4[ib][:, :], QMAX, -QMAX - 1.0, f_q4B[ib])

        def f_heavy(k):
            c, w0 = steps[k]
            ib = k % 2
            iu = k % 3
            rne2("vector", f_q2[ib][:, :], f_up[iu][:, :], scal(S_IS2),
                 [f_upB[iu], scB[S_IS2]], f_q2B[ib])
            clip_g(f_q2[ib][:, :], QMAX, -QMAX - 1.0, f_q2B[ib])
            E.op("vector", lambda e, ib=ib: e.tensor_tensor(
                f_q4[ib][:, :], f_q4[ib][:, :], f_q2[ib][:, :], ALU.mult),
                rd=[f_q2B[ib]], wr=[f_q4B[ib]])
            E.op("vector", lambda e, ib=ib: e.tensor_reduce(
                f_am[:, 1:2], f_q4[ib][:, :], AX.X, ALU.max,
                apply_absolute_value=True), rd=[f_q4B[ib]], wr=[f_amtB])
            E.op("vector", lambda e: e.tensor_tensor(
                f_am[:, 0:1], f_am[:, 0:1], f_am[:, 1:2], ALU.max),
                rd=[f_amtB], wr=[f_amB])
            E.dma("scalar", lambda e, c=c, w0=w0, ib=ib: e.dma_start(
                out=o_sp[c, :, w0:w0 + DW], in_=f_q4[ib][:, :]),
                rd=[f_q4B[ib]], wr=[o_spB])

        f_dma(0)
        for k in range(NS + 1):
            if k + 1 < NS:
                f_dma(k + 1)
            if k < NS:
                f_light(k)
            if k >= 1:
                f_heavy(k - 1)
            yield
        E.op("vector", lambda e: e.tensor_copy(scal(S_AM5), f_am[:, 0:1]),
             rd=[f_amB], wr=[scB[S_AM5]])

    THROT = {}

    def phaseATT():
        for c in range(NCD):
            ws = c % 2
            E.dma("gpsimd", lambda e, c=c, ws=ws: e.dma_start(
                out=t_wst[ws][:, :], in_=Wqkv_d[c * P:(c + 1) * P, 0:D]),
                wr=[t_wstB[ws]])
            split_pair(t_wst[ws][:, :],
                       wQh[:, c * 2 * D:c * 2 * D + D],
                       wQl[:, c * 2 * D:c * 2 * D + D],
                       [t_wstB[ws]], wQB, wQB)
            E.dma("gpsimd", lambda e, c=c, ws=ws: e.dma_start(
                out=t_wst[ws][:, :], in_=Wout_d[c * P:(c + 1) * P, :]),
                wr=[t_wstB[ws]])
            split_pair(t_wst[ws][:, :],
                       wQh[:, c * 2 * D + D:(c + 1) * 2 * D],
                       wQl[:, c * 2 * D + D:(c + 1) * 2 * D],
                       [t_wstB[ws]], wQB, wQB)

        def wq_q(t, kc, c):
            return t[:, kc * 2 * D + c * P:kc * 2 * D + (c + 1) * P]

        def wq_o(t, kc, half):
            b = kc * 2 * D + D
            return t[:, b + half * 512:b + (half + 1) * 512]

        def att_load(it):
            t0 = it * TA
            xv = it % 2
            thr = [THROT["tok"]] if "tok" in THROT else []
            E.dma("gpsimd", lambda e, t0=t0, xv=xv: e.dma_start(
                out=t_xnh[xv][:, :].rearrange("p (c t) -> p c t", c=NCD),
                in_=xnh_sp[:, :, t0:t0 + TA].rearrange("c p t -> p c t")),
                rd=[xn_spB], wr=[t_xnB[xv]], extra=thr)
            E.dma("gpsimd", lambda e, t0=t0, xv=xv: e.dma_start(
                out=t_xnl[xv][:, :].rearrange("p (c t) -> p c t", c=NCD),
                in_=xnl_sp[:, :, t0:t0 + TA].rearrange("c p t -> p c t")),
                rd=[xn_spB], wr=[t_xnB[xv]])

        att_load(0)
        for it in range(NTA):
            t0 = it * TA
            xv = it % 2
            if it + 1 < NTA:
                att_load(it + 1)
            bkn = 2
            for c in range(NCD):
                bk = c % 2
                fns = []
                for kc in range(NCD):
                    fns += mm3(bk, wq_q(wQh, kc, c), wq_q(wQl, kc, c),
                               t_xnh[xv][:, kc * TA:(kc + 1) * TA],
                               t_xnl[xv][:, kc * TA:(kc + 1) * TA],
                               first=(kc == 0), last=(kc == NCD - 1),
                               n0=0, n1=TA)
                E.emit("tensor", fns, rd=[wQB, t_xnB[xv]], wr=[pbB[bk]])
                E.op("scalar", lambda e, c=c, bk=bk: e.activation(
                    t_q[:, c * TA:(c + 1) * TA], pbank[bk][:, 0:TA],
                    AF.Identity, bias=bq_sb[:, c:c + 1], scale=1.0),
                    rd=[pbB[bk], BQB], wr=[t_qB[c]])
                E.op("scalar", lambda e, c=c: e.activation(
                    t_qsq[c % 2][:, :], t_q[:, c * TA:(c + 1) * TA],
                    AF.Square, bias=0.0, scale=1.0),
                    rd=[t_qB[c]], wr=[t_qsqB[c % 2]])
                if c >= 1:
                    E.op("tensor", lambda e, c=c, bkn=bkn: e.matmul(
                        pbank[bkn][0:16, 0:TA],
                        nmask_sb[:, (c - 1) * 16:c * 16],
                        t_qsq[(c - 1) % 2][:, :], start=(c == 1),
                        stop=False),
                        rd=[t_qsqB[(c - 1) % 2], NMB], wr=[pbB[bkn]])
                if c % 2 == 1:
                    yield False
            E.op("tensor", lambda e, bkn=bkn: e.matmul(
                pbank[bkn][0:16, 0:TA],
                nmask_sb[:, (NCD - 1) * 16:NCD * 16],
                t_qsq[(NCD - 1) % 2][:, :], start=False, stop=True),
                rd=[t_qsqB[(NCD - 1) % 2], NMB], wr=[pbB[bkn]])
            E.op("scalar", lambda e, bkn=bkn: e.activation(
                t_nsq[0:16, 0:TA], pbank[bkn][0:16, 0:TA], AF.Exp,
                bias=ln6_ap(16), scale=-1.0), rd=[pbB[bkn], CSTB], wr=[t_nsqB])
            E.dma("gpsimd", lambda e: e.dma_start(
                out=t_row[0:1, :].rearrange("p (h t) -> p h t", h=H),
                in_=t_nsq[0:16, 0:TA]), rd=[t_nsqB], wr=[t_rowB])
            def K(h):
                c = h // 2
                rlo = (h % 2) * 64
                qs = (h % 3) * 2 * TA
                qpb = 2 + (h % 2)
                for mc in range(2):
                    E.op("tensor", lambda e, c=c, rlo=rlo, mc=mc, qpb=qpb:
                         e.matmul(pbank[qpb][:, mc * TA:(mc + 1) * TA],
                                  omg_sb[rlo:rlo + 64, mc * P:(mc + 1) * P],
                                  t_q[rlo:rlo + 64, c * TA:(c + 1) * TA],
                                  start=True, stop=True),
                         rd=[t_qB[c], OMB], wr=[pbB[qpb]])
                E.op("scalar", lambda e, qpb=qpb, qs=qs:
                     e.activation(
                         t_qp[:, qs:qs + 2 * TA],
                         pbank[qpb][:, 0:2 * TA], AF.Exp, bias=0.0,
                         scale=1.0),
                     rd=[pbB[qpb]], wr=[t_qpB[h % 3]])

            def ND(h):
                plo = (h % 2) * 64
                numbk = 4 + (h % 4) // 2
                dbk = 6 + (h % 4) // 2
                dn0 = (h % 2) * TA
                qs = (h % 3) * 2 * TA
                fns = [lambda e, h=h, mc=mc, plo=plo, numbk=numbk, qs=qs:
                       e.matmul(
                           pbank[numbk][plo:plo + 64, 0:TA],
                           kvl_sb[:, (h * 2 + mc) * 65:(h * 2 + mc) * 65 + 64],
                           t_qp[:, qs + mc * TA:qs + (mc + 1) * TA],
                           start=(mc == 0), stop=(mc == 1)) for mc in range(2)]
                E.emit("tensor", fns, rd=[KVLB, t_qpB[h % 3]],
                       wr=[pbB[numbk]])
                fns = [lambda e, h=h, mc=mc, dbk=dbk, dn0=dn0, qs=qs:
                       e.matmul(
                           pbank[dbk][0:1, dn0:dn0 + TA],
                           kvl_sb[:, (h * 2 + mc) * 65 + 64:
                                  (h * 2 + mc) * 65 + 65],
                           t_qp[:, qs + mc * TA:qs + (mc + 1) * TA],
                           start=(h % 2 == 0 and mc == 0),
                           stop=(h % 2 == 1 and mc == 1)) for mc in range(2)]
                E.emit("tensor", fns, rd=[KVLB, t_qpB[h % 3]], wr=[pbB[dbk]])
                if h % 2 == 1:
                    p = h // 2
                    dbk2 = 6 + (h % 4) // 2
                    ds = (p % 2) * 2 * TA
                    E.op("vector", lambda e, h=h, dbk2=dbk2, ds=ds:
                         e.tensor_tensor(
                             t_den[0:1, ds:ds + 2 * TA],
                             pbank[dbk2][0:1, 0:2 * TA],
                             t_row[0:1, (h - 1) * TA:(h + 1) * TA], ALU.add),
                         rd=[pbB[dbk2], t_rowB], wr=[t_denB[p % 2]])
                    E.op("vector", lambda e, ds=ds: e.reciprocal(
                        t_den[0:1, ds:ds + 2 * TA],
                        t_den[0:1, ds:ds + 2 * TA]),
                        rd=[], wr=[t_denB[p % 2]])

            def PAIR2(p):
                numbk = 4 + (p % 2)
                bbk = p % 2
                ds = (p % 2) * 2 * TA
                fns = [lambda e, sub=sub, ds=ds, bbk=bbk: e.matmul(
                    pbank[bbk][sub * 64:(sub + 1) * 64, 0:TA],
                    ones_sb[0:1, 0:64],
                    t_den[0:1, ds + sub * TA:ds + (sub + 1) * TA],
                    start=True, stop=True) for sub in range(2)]
                E.emit("tensor", fns, rd=[ONB, t_denB[p % 2]],
                       wr=[pbB[bbk]])
                E.op("scalar", lambda e, numbk=numbk, p=p: e.activation(
                    t_num[p % 2][:, :], pbank[numbk][:, 0:TA],
                    AF.Identity, bias=0.0, scale=1.0),
                    rd=[pbB[numbk]], wr=[t_numB[p % 2]])
                E.op("vector", lambda e, p=p, bbk=bbk: e.tensor_tensor(
                    t_nd[:, p * TA:(p + 1) * TA], t_num[p % 2][:, :],
                    pbank[bbk][:, 0:TA], ALU.mult),
                    rd=[t_numB[p % 2], pbB[bbk]], wr=[t_ndB[p]])
                split_pair(t_nd[:, p * TA:(p + 1) * TA],
                           t_ndh[:, p * TA:(p + 1) * TA],
                           t_ndl[:, p * TA:(p + 1) * TA],
                           [t_ndB[p]], t_ndB[p], t_ndB[p])

            for h in range(H + 3):
                if h < H:
                    K(h)
                if 2 <= h <= H + 1:
                    ND(h - 2)
                if h >= 4 and h % 2 == 0:
                    PAIR2((h - 4) // 2)
                if h >= 2:
                    yield (h % 2 == 1) or (h > H + 1)
            for s2 in range(2):
                ab = s2 % 2
                for half in range(2):
                    zbk = s2 * 2 + half
                    fns = []
                    for c2 in range(NCD):
                        fns += mm3(
                            zbk,
                            t_ndh[:, c2 * TA + s2 * P:
                                       c2 * TA + (s2 + 1) * P],
                            t_ndl[:, c2 * TA + s2 * P:
                                       c2 * TA + (s2 + 1) * P],
                            wq_o(wQh, c2, half), wq_o(wQl, c2, half),
                            first=(c2 == 0), last=False)
                    fns.append(lambda e, half=half, zbk=zbk: e.matmul(
                        pbank[zbk][:, :], ones_sb[0:1, 0:P],
                        bo_sb[0:1, half * 512:(half + 1) * 512],
                        start=False, stop=True))
                    E.emit("tensor", fns, rd=t_ndB + [wQB, ONB, BOB],
                           wr=[pbB[zbk]])
                    E.op("scalar", lambda e, ab=ab, half=half, zbk=zbk:
                         e.activation(
                             t_att[ab][:, half * 512:(half + 1) * 512],
                             pbank[zbk][:, :], AF.Identity, bias=0.0,
                             scale=1.0), rd=[pbB[zbk]], wr=[t_attB[ab]])
                E.dma("scalar", lambda e, t0=t0, s2=s2, ab=ab: e.dma_start(
                    out=att_sp[t0 + s2 * P:t0 + (s2 + 1) * P, :],
                    in_=t_att[ab][:, :]), rd=[t_attB[ab]], wr=[att_spB])
                if s2 == 0:
                    yield True
            yield True

    genD = phaseD()
    genATT = phaseATT()
    genF = phaseF()

    def drain2(g1, g2, r1, r2, throttle=False, skip=0):
        done1 = done2 = False
        safe = True
        it = 0
        while not done1:
            it += 1
            for _ in range(r1):
                try:
                    next(g1)
                except StopIteration:
                    done1 = True
                    break
            if throttle:
                THROT["tok"] = ("V", E.cnt["V"])
            if not done2 and not (skip and it % skip == 0):
                for _ in range(r2):
                    try:
                        safe = next(g2)
                    except StopIteration:
                        done2 = True
                        safe = True
                        break
        while not safe and not done2:
            try:
                safe = next(g2)
            except StopIteration:
                done2 = True
                safe = True
        return done2

    att_done = drain2(genD, genATT, 1, 1, skip=10)
    ATTS = {"done": att_done}

    def adv_att(n):
        for _ in range(n):
            if ATTS["done"]:
                return
            try:
                next(genATT)
            except StopIteration:
                ATTS["done"] = True

    global_absmax(2, [S_AM4], adv=adv_att)
    sc_ts(S_TMP1, S_S3, 1.0 / 65536.0, ALU.mult)
    sc_mul(S_S4, S_AM4, S_TMP1)
    sc_ts(S_S4, S_S4, 1.0 / QMAX, ALU.mult, 1e-12, ALU.max)
    sc_recip(S_C4, S_S4)
    sc_mul(S_C4, S_C4, S_TMP1)

    if not ATTS["done"]:
        ATTS["done"] = drain2(genF, genATT, 1, 1, throttle=True)
    else:
        for _ in genF:
            pass
    if not ATTS["done"]:
        for _ in genATT:
            pass
    global_absmax(3, [S_AM5])
    sc_mul(S_TMP1, S_S4, S_S2)
    sc_mul(S_S5, S_AM5, S_TMP1)
    sc_ts(S_S5, S_S5, 1.0 / QMAX, ALU.mult, 1e-12, ALU.max)
    sc_recip(S_C5, S_S5)
    sc_mul(S_C5, S_C5, S_TMP1)

    fence = E.fence()
    refence([wPB, wQB], fence)
    E.dma("sync", lambda e: e.dma_start(
        out=wPh[:, :].rearrange("p (c o) -> p c o", c=NCI),
        in_=wdTh_sp.rearrange("c p o -> p c o")), rd=[wdT_spB], wr=[wPB])
    E.dma("sync", lambda e: e.dma_start(
        out=wPl[:, :].rearrange("p (c o) -> p c o", c=NCI),
        in_=wdTl_sp.rearrange("c p o -> p c o")), rd=[wdT_spB], wr=[wPB])

    alg = mkplace((WPB_B, WQB_B))
    arG = mkar()
    g_o = [alg([P, NCI * P], F32) for _ in range(2)]
    g_oB = [Buf("g_o0", fence), Buf("g_o1", fence)]
    g_q5f = alg([P, NCI * P], F32)
    g_q5h = [alg([P, NCI * P], BF16) for _ in range(2)]
    g_q5l = [alg([P, NCI * P], BF16) for _ in range(2)]
    g_q5fB = Buf("g_q5f", fence)
    g_q5B = [Buf("g_q5a", fence), Buf("g_q5b", fence)]
    g_x = [arG(D) for _ in range(2)]
    g_xB = [Buf("g_x0", fence), Buf("g_x1", fence)]
    g_at = [arG(D) for _ in range(2)]
    g_atB = [Buf("g_at0", fence), Buf("g_at1", fence)]
    g_out = [arG(D) for _ in range(2)]
    g_outB = [Buf("g_out0", fence), Buf("g_out1", fence)]

    def g_chain(it):
        t0 = it * P
        ob = it % 2
        E.dma("sync", lambda e, t0=t0, ob=ob: e.dma_start(
            out=g_o[ob][:, :].rearrange("p (c t) -> p c t", c=NCI),
            in_=o_sp[:, :, t0:t0 + P].rearrange("c p t -> p c t")),
            rd=[o_spB], wr=[g_oB[ob]])
        E.dma("gpsimd", lambda e, t0=t0, ob=ob: e.dma_start(
            out=g_x[ob][:, :], in_=x_d[t0:t0 + P, :]), wr=[g_xB[ob]])
        E.dma("gpsimd", lambda e, t0=t0, ob=ob: e.dma_start(
            out=g_at[ob][:, :], in_=att_sp[t0:t0 + P, :]),
            rd=[att_spB], wr=[g_atB[ob]])
        rne2("vector", g_q5f[:, :], g_o[ob][:, :], scal(S_C5),
             [g_oB[ob], scB[S_C5]], g_q5fB)
        clip_g(g_q5f[:, :], QMAX, -QMAX - 1.0, g_q5fB)
        split_pair(g_q5f[:, :], g_q5h[ob][:, :], g_q5l[ob][:, :],
                   [g_q5fB], g_q5B[ob], g_q5B[ob])

    g_chain(0)
    for it in range(NTG):
        t0 = it * P
        ob = it % 2
        if it + 1 < NTG:
            g_chain(it + 1)
        for half in range(2):
            bk = (it % 2) * 2 + half
            fns = []
            for ic in range(NCI):
                fns += mm3(bk,
                           g_q5h[ob][:, ic * P:(ic + 1) * P],
                           g_q5l[ob][:, ic * P:(ic + 1) * P],
                           wPh[:, ic * D + half * 512:
                                    ic * D + (half + 1) * 512],
                           wPl[:, ic * D + half * 512:
                                    ic * D + (half + 1) * 512],
                           first=(ic == 0), last=(ic == NCI - 1))
            E.emit("tensor", fns, rd=[wPB, g_q5B[ob]], wr=[pbB[bk]])
            E.op("scalar", lambda e, ob=ob, half=half, bk=bk: e.activation(
                g_out[ob][:, half * 512:(half + 1) * 512], pbank[bk][:, :],
                AF.Identity, bias=0.0, scale=scal(S_S5)),
                rd=[pbB[bk], scB[S_S5]], wr=[g_outB[ob]])
        E.op("vector", lambda e, ob=ob: e.tensor_tensor(
            g_out[ob][:, :], g_out[ob][:, :], g_x[ob][:, :], ALU.add),
            rd=[g_xB[ob]], wr=[g_outB[ob]])
        E.op("vector", lambda e, ob=ob: e.tensor_tensor(
            g_out[ob][:, :], g_out[ob][:, :], g_at[ob][:, :], ALU.add),
            rd=[g_atB[ob]], wr=[g_outB[ob]])
        E.dma("scalar", lambda e, t0=t0, ob=ob: e.dma_start(
            out=out_d[t0:t0 + P, :], in_=g_out[ob][:, :]), rd=[g_outB[ob]])

    fin = [(f"AD{i}", E.cnt[f"AD{i}"]) for i in range(NSLOT)
           if E.cnt[f"AD{i}"] > 0]
    E.op("gpsimd", lambda e: e.nop(), extra=fin)

    sems = {s: st.enter_context(nc.semaphore(s)) for s in ALL_SEMS}
    block = st.enter_context(nc.Block())
    E.run(block, sems)
    st.close()
    return nc


_CACHE = {}


def _get_nc(T):
    if T not in _CACHE:
        _CACHE[T] = build_nc(T)
    return _CACHE[T]


def kernel(L0_attn_x, L0_attn_Wqkv, L0_attn_bqkv, L0_attn_Wout, L0_attn_bout,
           L0_attn_omega, L0_ffn_w_gate, L0_ffn_w_up, L0_ffn_w_down):
    x = np.ascontiguousarray(np.asarray(L0_attn_x, dtype=np.float32))
    B, T, Dm = x.shape
    assert Dm == D
    ident, nmask = _host_consts()
    nc = _get_nc(T)
    common = {
        "Wqkv": np.ascontiguousarray(L0_attn_Wqkv, dtype=np.float32),
        "bqkv": np.ascontiguousarray(L0_attn_bqkv, dtype=np.float32),
        "Wout": np.ascontiguousarray(L0_attn_Wout, dtype=np.float32),
        "bout": np.ascontiguousarray(L0_attn_bout, dtype=np.float32),
        "omega": np.ascontiguousarray(L0_attn_omega, dtype=np.float32),
        "w_gate": np.ascontiguousarray(L0_ffn_w_gate, dtype=np.float32),
        "w_up": np.ascontiguousarray(L0_ffn_w_up, dtype=np.float32),
        "w_down": np.ascontiguousarray(L0_ffn_w_down, dtype=np.float32),
        "ident": ident, "nmask": nmask,
    }
    in_maps = [dict(common, x=np.ascontiguousarray(x[i])) for i in range(B)]
    res = run_bass_kernel_spmd(nc, in_maps, core_ids=list(range(B)))
    return np.stack([np.asarray(res.results[i]["out"]) for i in range(B)],
                    axis=0)
